# version 02
# baseline (speedup 1.0000x reference)
import contextlib
import numpy as np
import concourse.bass as bass
import concourse.mybir as mybir
from concourse.bass_utils import run_bass_kernel_spmd

F32 = mybir.dt.float32
BF16 = mybir.dt.bfloat16
AF = mybir.ActivationFunctionType
ALU = mybir.AluOpType
AX = mybir.AxisListType

MAGIC = 12582912.0
ENGINES = ("sync", "vector", "scalar", "tensor", "gpsimd")
D = 1024
NCH = D // 128


class Ctr:
    def __init__(self, es, nc, name):
        self.h = es.enter_context(nc.semaphore(name))
        self.n = 0

    def inc(self, amt=1):
        self.n += amt
        return self.n


class Prog:
    def __init__(self, nc):
        self.nc = nc
        self.steps = {k: [] for k in ENGINES}

    def sy(self, fn): self.steps["sync"].append(fn)
    def v(self, fn): self.steps["vector"].append(fn)
    def s(self, fn): self.steps["scalar"].append(fn)
    def t(self, fn): self.steps["tensor"].append(fn)
    def g(self, fn): self.steps["gpsimd"].append(fn)

    def emit(self, block):
        for key in ENGINES:
            fns = self.steps[key]
            if not fns:
                continue
            dec = getattr(block, key)

            def make(fns):
                def _(eng):
                    for f in fns:
                        f(eng)
                return _
            dec(make(fns))


def wge(e, h, n):
    if n:
        e.wait_ge(h, n)


class Arena:
    def __init__(self, nc, es, name, dtype, n_elems):
        self.t = es.enter_context(nc.sbuf_tensor(name, [128, n_elems], dtype))
        self.n = n_elems
        self.off = 0

    def carve(self, n):
        n_al = (n + 31) & ~31
        assert self.off + n_al <= self.n, f"arena overflow {self.off}+{n_al}>{self.n}"
        ap = self.t[:, self.off:self.off + n]
        self.off += n_al
        return ap

    def reset(self):
        self.off = 0


def build_consts(nc, es, P, gch):
    ident_f32 = es.enter_context(nc.sbuf_tensor("ident_f32", [128, 128], F32))
    ident_bf = es.enter_context(nc.sbuf_tensor("ident_bf", [128, 128], BF16))
    ones_bf = es.enter_context(nc.sbuf_tensor("ones_bf", [128, 1], BF16))
    ones_row = es.enter_context(nc.sbuf_tensor("ones_row", [1, 128], F32))

    def g_fn(g):
        g.memset(ident_bf[:], 1.0).then_inc(gch.h, 1)
        g.memset(ident_f32[:], 1.0).then_inc(gch.h, 1)
        g.memset(ones_bf[:], 1.0)
        g.memset(ones_row[:], 1.0)
        g.wait_ge(gch.h, 2)
        g.affine_select(out=ident_bf[:], in_=ident_bf[:], pattern=[[-1, 128]],
                        compare_op=ALU.is_equal, fill=0.0,
                        base=0, channel_multiplier=1)
        g.affine_select(out=ident_f32[:], in_=ident_f32[:], pattern=[[-1, 128]],
                        compare_op=ALU.is_equal, fill=0.0,
                        base=0, channel_multiplier=1).then_inc(gch.h, 1)
    P.g(g_fn)
    gch.inc(3)
    return dict(ident_bf=ident_bf, ident_f32=ident_f32, ones_bf=ones_bf,
                ones_row=ones_row, ready=gch.n)


def load_weights_bf(P, stage, names, ldw, vch):
    out = []
    prev_cast = 0
    for (wbf, ap, c_out) in names:
        src = ap.rearrange("(n p) m -> p n m", p=128)
        t_ld = ldw.inc(16)

        def sy_fn(e, src=src, c_out=c_out, prev_cast=prev_cast):
            wge(e, vch.h, prev_cast)
            e.dma_start(
                out=stage[:, 0:NCH * c_out].rearrange("p (n m) -> p n m", n=NCH),
                in_=src).then_inc(ldw.h, 16)
        P.sy(sy_fn)
        t_cast = vch.inc(1)

        def v_fn(e, wbf=wbf, c_out=c_out, t_ld=t_ld):
            e.wait_ge(ldw.h, t_ld)
            e.tensor_copy(out=wbf, in_=stage[:, 0:NCH * c_out]).then_inc(vch.h, 1)
        P.v(v_fn)
        out.append(t_cast)
        prev_cast = t_cast
    return out


def emit_ln_blocks(P, bufs, vch, sch, *, src, dst, blocks, war_s=None):
    stats, mv, veps, rstd, negmr = (bufs[k] for k in
                                    ("stats", "mv", "veps", "rstd", "negmr"))
    nblk = len(blocks)

    def v_stats(e):
        for j, (sc, dc, r) in enumerate(blocks):
            e.bn_stats(out=stats[0:r, 12 * j:12 * j + 6],
                       in_=src[0:r, sc:sc + 512])
            ins = e.bn_stats(out=stats[0:r, 12 * j + 6:12 * j + 12],
                             in_=src[0:r, sc + 512:sc + D])
        ins.then_inc(vch.h, 1)
    P.v(v_stats)
    t_stats = vch.inc(1)

    def v_aggr(e):
        e.wait_ge(vch.h, t_stats)
        for j, (sc, dc, r) in enumerate(blocks):
            ins = e.bn_aggr(out=mv[0:r, 2 * j:2 * j + 2],
                            in_=stats[0:r, 12 * j:12 * j + 12])
        ins.then_inc(vch.h, 1)
    P.v(v_aggr)
    t_aggr = vch.inc(1)

    def v_veps(e):
        e.wait_ge(vch.h, t_aggr)
        for j, (sc, dc, r) in enumerate(blocks):
            ins = e.tensor_scalar_add(out=veps[0:r, j:j + 1],
                                      in0=mv[0:r, 2 * j + 1:2 * j + 2], scalar1=1e-5)
        ins.then_inc(vch.h, 1)
    P.v(v_veps)
    t_veps = vch.inc(1)

    maxr = max(r for (_, _, r) in blocks)

    def s_ln(e):
        e.wait_ge(vch.h, t_veps)
        e.activation(out=rstd[0:maxr, 0:nblk], in_=veps[0:maxr, 0:nblk],
                     func=AF.Ln).then_inc(sch.h, 1)
    P.s(s_ln)
    t_ln = sch.inc(1)

    def s_exp(e):
        e.wait_ge(sch.h, t_ln)
        e.activation(out=rstd[0:maxr, 0:nblk], in_=rstd[0:maxr, 0:nblk],
                     func=AF.Exp, scale=-0.5).then_inc(sch.h, 1)
    P.s(s_exp)
    t_rstd = sch.inc(1)

    def v_negmr(e):
        e.wait_ge(sch.h, t_rstd)
        for j, (sc, dc, r) in enumerate(blocks):
            ins = e.scalar_tensor_tensor(out=negmr[0:r, j:j + 1],
                                         in0=mv[0:r, 2 * j:2 * j + 1], scalar=-1.0,
                                         in1=rstd[0:r, j:j + 1],
                                         op0=ALU.mult, op1=ALU.mult)
        ins.then_inc(vch.h, 1)
    P.v(v_negmr)
    t_negmr = vch.inc(1)

    def s_apply(e):
        e.wait_ge(vch.h, t_negmr)
        if war_s is not None:
            h, n = war_s
            wge(e, h, n)
        for j, (sc, dc, r) in enumerate(blocks):
            ins = e.activation(out=dst[0:r, dc:dc + D],
                               in_=src[0:r, sc:sc + D],
                               func=AF.Identity, bias=negmr[0:r, j:j + 1],
                               scale=rstd[0:r, j:j + 1])
        ins.then_inc(sch.h, 1)
    P.s(s_apply)
    return sch.inc(1)


def emit_ln(P, bufs, vch, sch, *, nblk, src, srcol, dst, dstcol, war_s=None):
    blocks = [(srcol(j), dstcol(j), 128) for j in range(nblk)]
    return emit_ln_blocks(P, bufs, vch, sch, src=src, dst=dst, blocks=blocks,
                          war_s=war_s)


def mk_ln_bufs(nc, es, maxblk):
    return dict(
        stats=es.enter_context(nc.sbuf_tensor("lnstats", [128, 12 * maxblk], F32)),
        mv=es.enter_context(nc.sbuf_tensor("lnmv", [128, 2 * maxblk], F32)),
        veps=es.enter_context(nc.sbuf_tensor("lnveps", [128, maxblk], F32)),
        rstd=es.enter_context(nc.sbuf_tensor("lnrstd", [128, maxblk], F32)),
        negmr=es.enter_context(nc.sbuf_tensor("lnnegmr", [128, maxblk], F32)),
    )



def build_l0attn(T_own, T_full, QG=512, KSC=512):
    nc = bass.Bass("TRN2", target_bir_lowering=False)
    x_full = nc.dram_tensor("x_full", [T_full, D], F32, kind="ExternalInput").ap()
    x_own = nc.dram_tensor("x_own", [T_own, D], F32, kind="ExternalInput").ap()
    wqT = nc.dram_tensor("wqT", [D, D], F32, kind="ExternalInput").ap()
    wkT = nc.dram_tensor("wkT", [D, D], F32, kind="ExternalInput").ap()
    wvT = nc.dram_tensor("wvT", [D, D], F32, kind="ExternalInput").ap()
    s1 = nc.dram_tensor("s1", [T_own, D], F32, kind="ExternalOutput").ap()

    kfm_d = nc.dram_tensor("kfm_d", [NCH, 128, T_full], BF16, kind="Internal")
    vtm_d = nc.dram_tensor("vtm_d", [T_full, 1025], BF16, kind="Internal")
    qfm_d = nc.dram_tensor("qfm_d", [NCH, 128, T_own], BF16, kind="Internal")

    TILE = 512
    NBLK = TILE // 128
    nt_full = T_full // TILE
    nt_own = T_own // TILE
    n_ksc = T_full // KSC
    n_kc = KSC // 128
    n_groups = T_own // QG
    NSUB = QG // 128
    NCHU_F = T_full // 128

    es = contextlib.ExitStack()
    P = Prog(nc)

    vch = Ctr(es, nc, "vch")
    sch = Ctr(es, nc, "sch")
    tch = Ctr(es, nc, "tch")
    gch = Ctr(es, nc, "gch")
    ldw = Ctr(es, nc, "ldw")
    ldx = [Ctr(es, nc, f"ldx{p}") for p in range(2)]
    splp = [Ctr(es, nc, f"splp{p}") for p in range(2)]
    splv = [Ctr(es, nc, f"splv{p}") for p in range(2)]
    ldqg = Ctr(es, nc, "ldqg")
    ldkv = [Ctr(es, nc, f"ldkv{p}") for p in range(2)]
    ldxr = Ctr(es, nc, "ldxr")
    st = Ctr(es, nc, "st")

    consts = build_consts(nc, es, P, gch)
    ident_f32 = consts["ident_f32"]
    ones_bf = consts["ones_bf"]
    ones_row = consts["ones_row"]
    t_consts = consts["ready"]

    abf = Arena(nc, es, "abf", BF16, 56384)
    af32 = Arena(nc, es, "af32", F32, 16448)

    wq_bf = abf.carve(NCH * D)
    wk_bf = abf.carve(NCH * D)
    wv_bf = abf.carve(NCH * D)
    xfm = [abf.carve(NCH * TILE) for _ in range(2)]
    prj = [abf.carve(NCH * TILE) for _ in range(2)]
    vsb = [abf.carve(NBLK * 1025) for _ in range(2)]
    a1_bf_end = abf.off
    xst = [af32.carve(NBLK * D) for _ in range(2)]
    wstage = af32.carve(NCH * D)
    a1_f32_end = af32.off

    ksq = [es.enter_context(nc.sbuf_tensor(f"ksq{p}", [128, 512], BF16)) for p in range(2)]
    lnb = mk_ln_bufs(nc, es, 4)
    normq = es.enter_context(nc.sbuf_tensor("normq", [1, max(T_own, 512)], F32))
    nkv = es.enter_context(nc.sbuf_tensor("nkv", [1, 512], F32))
    nk1 = es.enter_context(nc.sbuf_tensor("nk1", [1, 1], F32))
    mkrun = es.enter_context(nc.sbuf_tensor("mkrun", [1, 1], F32))
    mksc = es.enter_context(nc.sbuf_tensor("mksc", [1, 1], F32))
    m_q = normq
    tsb = [es.enter_context(nc.sbuf_tensor(f"tsb{p}", [128, QG], F32)) for p in range(2)]
    m_bc = es.enter_context(nc.sbuf_tensor("m_bc", [128, QG], F32))
    rden = es.enter_context(nc.sbuf_tensor("rden", [128, 4], F32))

    W = load_weights_bf(P, wstage, [(wq_bf, wqT, D), (wk_bf, wkT, D),
                                    (wv_bf, wvT, D)], ldw, vch)
    t_wq, t_wk, t_wv = W

    psA = es.enter_context(nc.psum_tensor("psA", [128, 512], F32))
    psB = es.enter_context(nc.psum_tensor("psB", [128, 512], F32))
    psO0 = es.enter_context(nc.psum_tensor("psO0", [128, 1025], F32))
    psO1 = es.enter_context(nc.psum_tensor("psO1", [128, 1025], F32))
    psAB = [psA, psB]
    psO = [psO0, psO1]

    def v_init(e):
        e.memset(vsb[0][:, 1024::1025], 1.0)
        e.memset(vsb[1][:, 1024::1025], 1.0)
        e.memset(mkrun[:], 0.0).then_inc(vch.h, 1)
    P.v(v_init)
    t_vinit = vch.inc(1)

    bank_free = {0: 0, 1: 0}
    state = {}
    apply_done = {}

    def emit_tile(i, gi, src_ap, pass_kind):
        p = gi % 2
        t_apfree = apply_done.get(gi - 2, 0)
        t_tp_prev = state.get(f"tp_{gi - 2}", 0)
        src = src_ap[i * TILE:(i + 1) * TILE, :].rearrange("(j p) c -> p j c", p=128)

        def sy_load(e, src=src, p=p, t_apfree=t_apfree, t_tp_prev=t_tp_prev):
            wge(e, sch.h, t_apfree)
            wge(e, tch.h, t_tp_prev)
            e.dma_start(out=xst[p].rearrange("p (j c) -> p j c", j=NBLK),
                        in_=src).then_inc(ldx[p].h, 16)
        P.sy(sy_load)
        t_ldx = ldx[p].inc(16)

        def v_predma(e, p=p, t_ldx=t_ldx):
            e.wait_ge(ldx[p].h, t_ldx)
        P.v(v_predma)
        t_apply = emit_ln(P, lnb, vch, sch, nblk=NBLK,
                          src=xst[p], srcol=lambda j: j * D,
                          dst=xst[p], dstcol=lambda j: j * D,
                          war_s=(tch.h, t_tp_prev))
        apply_done[gi] = t_apply

        for j in range(NBLK):
            for h in range(2):
                bk = (j * 2 + h) % 2
                t_bankfree = bank_free[bk]

                def t_tp(e, p=p, j=j, h=h, bk=bk, t_bankfree=t_bankfree,
                         t_ap=t_apply):
                    e.wait_ge(sch.h, t_ap)
                    wge(e, gch.h, t_consts)
                    wge(e, vch.h, t_bankfree)
                    for cc in range(4):
                        c = 4 * h + cc
                        ins = e.transpose(
                            out=psAB[bk][:, cc * 128:(cc + 1) * 128],
                            in_=xst[p][:, j * D + c * 128:j * D + (c + 1) * 128],
                            identity=ident_f32[:])
                    ins.then_inc(tch.h, 1)
                P.t(t_tp)
                t_tpg = tch.inc(1)

                def v_evac_tp(e, p=p, j=j, h=h, bk=bk, t_tpg=t_tpg):
                    e.wait_ge(tch.h, t_tpg)
                    dst = xfm[p].rearrange("p (c t) -> p c t", c=NCH)
                    e.tensor_copy(out=dst[:, 4 * h:4 * h + 4, j * 128:(j + 1) * 128],
                                  in_=psAB[bk][:].rearrange("p (c t) -> p c t", c=4)
                                  ).then_inc(vch.h, 1)
                P.v(v_evac_tp)
                bank_free[bk] = vch.inc(1)
        state[f"tp_{gi}"] = tch.n
        t_xfm_ready = vch.n

        if pass_kind == "kv":
            plist = [("k", wk_bf, t_wk, None), ("v", wv_bf, t_wv, None)]
        else:
            plist = [("q", wq_bf, t_wq, 1.0 / 32.0)]

        for (nm, wbf, t_w, scale) in plist:
            if nm in ("k", "q"):
                spl_sem = splp[p]
                t_spl_prev = state.get(f"splp_done_{gi - 2}", 0)
                for e_ch in range(NCH):
                    bk = e_ch % 2
                    t_bankfree = bank_free[bk]

                    def t_proj(e, p=p, e_ch=e_ch, bk=bk, wbf=wbf, t_w=t_w,
                               t_bankfree=t_bankfree, t_x=t_xfm_ready):
                        e.wait_ge(vch.h, max(t_x, t_w, t_bankfree))
                        for c in range(NCH):
                            ins = e.matmul(
                                out=psAB[bk][:, 0:TILE],
                                lhsT=wbf[:, c * D + e_ch * 128:c * D + (e_ch + 1) * 128],
                                rhs=xfm[p][:, c * TILE:(c + 1) * TILE],
                                start=(c == 0), stop=(c == NCH - 1))
                        ins.then_inc(tch.h, 1)
                    P.t(t_proj)
                    t_mm = tch.inc(1)

                    def v_evac(e, p=p, e_ch=e_ch, bk=bk, scale=scale, t_mm=t_mm,
                               spl_sem=spl_sem, t_spl_prev=t_spl_prev):
                        e.wait_ge(tch.h, t_mm)
                        if e_ch == 0:
                            wge(e, spl_sem.h, t_spl_prev)
                        if scale is None:
                            e.tensor_copy(out=prj[p][:, e_ch * TILE:(e_ch + 1) * TILE],
                                          in_=psAB[bk][:, 0:TILE]).then_inc(vch.h, 1)
                        else:
                            e.tensor_scalar_mul(out=prj[p][:, e_ch * TILE:(e_ch + 1) * TILE],
                                                in0=psAB[bk][:, 0:TILE],
                                                scalar1=scale).then_inc(vch.h, 1)
                    P.v(v_evac)
                    bank_free[bk] = vch.inc(1)
                    t_prj_e = bank_free[bk]

                    kq = ksq[e_ch % 2]
                    t_nk_prev = state.get(f"nk_{(e_ch - 2) % NCH}", 0)

                    def v_ksq(e, p=p, e_ch=e_ch, kq=kq, t_prj_e=t_prj_e,
                              t_nk_prev=t_nk_prev):
                        e.wait_ge(vch.h, t_prj_e)
                        wge(e, tch.h, t_nk_prev)
                        e.tensor_tensor(out=kq[:, 0:TILE],
                                        in0=prj[p][:, e_ch * TILE:(e_ch + 1) * TILE],
                                        in1=prj[p][:, e_ch * TILE:(e_ch + 1) * TILE],
                                        op=ALU.mult).then_inc(vch.h, 1)
                    P.v(v_ksq)
                    t_ksq = vch.inc(1)

                    t_o0 = state.get("t_o0_free", 0)

                    def t_nk(e, e_ch=e_ch, kq=kq, t_ksq=t_ksq, t_o0=t_o0):
                        e.wait_ge(vch.h, t_ksq)
                        wge(e, gch.h, t_consts)
                        if e_ch == 0:
                            wge(e, vch.h, t_o0)
                        e.matmul(out=psO0[0:1, 0:TILE], lhsT=ones_bf[:],
                                 rhs=kq[:, 0:TILE], start=(e_ch == 0),
                                 stop=(e_ch == NCH - 1)).then_inc(tch.h, 1)
                    P.t(t_nk)
                    state[f"nk_{e_ch}"] = tch.inc(1)
                t_nk_last = state[f"nk_{NCH - 1}"]

                if pass_kind == "q":
                    def v_nk_evac(e, i=i, t_nk_last=t_nk_last):
                        e.wait_ge(tch.h, t_nk_last)
                        e.tensor_copy(out=normq[0:1, i * TILE:(i + 1) * TILE],
                                      in_=psO0[0:1, 0:TILE]).then_inc(vch.h, 1)
                    P.v(v_nk_evac)
                    state["t_o0_free"] = vch.inc(1)
                else:
                    t_red_prev = state.get("t_nk_red", 0)

                    def v_nk_evac(e, t_nk_last=t_nk_last, t_red_prev=t_red_prev):
                        e.wait_ge(tch.h, t_nk_last)
                        wge(e, vch.h, t_red_prev)
                        e.tensor_copy(out=nkv[:], in_=psO0[0:1, 0:TILE]
                                      ).then_inc(vch.h, 1)
                    P.v(v_nk_evac)
                    state["t_o0_free"] = vch.inc(1)
                    t_ev = state["t_o0_free"]

                    def v_nk_red(e, t_ev=t_ev, first=(i == 0)):
                        e.wait_ge(vch.h, t_ev)
                        e.tensor_reduce(out=nk1[:], in_=nkv[:], axis=AX.X,
                                        op=ALU.max).then_inc(vch.h, 1)
                    P.v(v_nk_red)
                    t_r1 = vch.inc(1)

                    def v_nk_max(e, t_r1=t_r1):
                        e.wait_ge(vch.h, t_r1)
                        wge(e, vch.h, t_vinit)
                        e.tensor_tensor(out=mkrun[:], in0=mkrun[:], in1=nk1[:],
                                        op=ALU.max).then_inc(vch.h, 1)
                    P.v(v_nk_max)
                    state["t_nk_red"] = vch.inc(1)

                dd = kfm_d if pass_kind == "kv" else qfm_d
                dst = dd.ap().transpose([1, 0, 2])[:, :, i * TILE:(i + 1) * TILE]
                t_all_evac = t_prj_e

                def sy_spill(e, p=p, dst=dst, t_all_evac=t_all_evac,
                             spl_sem=spl_sem):
                    e.wait_ge(vch.h, t_all_evac)
                    e.dma_start(out=dst,
                                in_=prj[p].rearrange("p (c t) -> p c t", c=NCH)
                                ).then_inc(spl_sem.h, 16)
                P.sy(sy_spill)
                state[f"splp_done_{gi}"] = spl_sem.inc(16)
            else:
                t_splv_prev = state.get(f"splv_done_{gi - 2}", 0)
                for j in range(NBLK):
                    for hh in range(2):
                        bk = (j * 2 + hh) % 2
                        t_bankfree = bank_free[bk]

                        def t_vmm(e, p=p, j=j, hh=hh, bk=bk, wbf=wbf,
                                  t_bankfree=t_bankfree, t_x=t_xfm_ready, t_w=t_w):
                            e.wait_ge(vch.h, max(t_x, t_w, t_bankfree))
                            for c in range(NCH):
                                ins = e.matmul(
                                    out=psAB[bk][:, 0:512],
                                    lhsT=xfm[p][:, c * TILE + j * 128:c * TILE + (j + 1) * 128],
                                    rhs=wbf[:, c * D + hh * 512:c * D + (hh + 1) * 512],
                                    start=(c == 0), stop=(c == NCH - 1))
                            ins.then_inc(tch.h, 1)
                        P.t(t_vmm)
                        t_mm = tch.inc(1)

                        def v_vevac(e, p=p, j=j, hh=hh, bk=bk, t_mm=t_mm,
                                    t_splv_prev=t_splv_prev):
                            e.wait_ge(tch.h, t_mm)
                            wge(e, vch.h, t_vinit)
                            if j == 0 and hh == 0:
                                wge(e, splv[p].h, t_splv_prev)
                            e.tensor_copy(
                                out=vsb[p][:, j * 1025 + hh * 512:j * 1025 + (hh + 1) * 512],
                                in_=psAB[bk][:, 0:512]).then_inc(vch.h, 1)
                        P.v(v_vevac)
                        bank_free[bk] = vch.inc(1)
                t_v_evac = bank_free[(NBLK * 2 - 1) % 2]
                dstv = vtm_d.ap()[i * TILE:(i + 1) * TILE, :].rearrange(
                    "(j p) c -> p j c", p=128)

                def sy_vspill(e, p=p, dstv=dstv, t_v_evac=t_v_evac):
                    e.wait_ge(vch.h, t_v_evac)
                    e.dma_start(out=dstv,
                                in_=vsb[p].rearrange("p (j c) -> p j c", j=NBLK)
                                ).then_inc(splv[p].h, 16)
                P.sy(sy_vspill)
                state[f"splv_done_{gi}"] = splv[p].inc(16)

    gi = 0
    for i in range(nt_full):
        emit_tile(i, gi, x_full, "kv")
        gi += 1
    for i in range(nt_own):
        emit_tile(i, gi, x_own, "q")
        gi += 1
    t_a1_tensor_end = tch.n
    t_a1_scalar_end = sch.n

    t_normq_last = state["t_o0_free"]
    t_mk_last = state["t_nk_red"]

    def v_mk2(e):
        e.wait_ge(vch.h, max(t_normq_last, t_mk_last))
        e.tensor_scalar_max(out=mkrun[:], in0=mkrun[:], scalar1=1e-19)
        e.tensor_scalar_max(out=normq[0:1, 0:T_own], in0=normq[0:1, 0:T_own],
                            scalar1=1e-19).then_inc(vch.h, 1)
    P.v(v_mk2)
    t_mk2 = vch.inc(1)

    def s_mk3(e):
        e.wait_ge(vch.h, t_mk2)
        e.activation(out=mkrun[:], in_=mkrun[:], func=AF.Ln).then_inc(sch.h, 1)
    P.s(s_mk3)
    t_mk3 = sch.inc(1)

    def v_mk4(e):
        e.wait_ge(sch.h, t_mk3)
        e.tensor_scalar_mul(out=mksc[:], in0=mkrun[:], scalar1=0.5).then_inc(vch.h, 1)
    P.v(v_mk4)
    t_mk4 = vch.inc(1)

    def s_mk5(e):
        e.wait_ge(vch.h, t_mk4)
        e.activation(out=normq[0:1, 0:T_own], in_=normq[0:1, 0:T_own],
                     func=AF.Ln).then_inc(sch.h, 1)
    P.s(s_mk5)
    t_mk5 = sch.inc(1)

    def s_mk6(e):
        e.wait_ge(sch.h, t_mk5)
        e.activation(out=m_q[0:1, 0:T_own], in_=normq[0:1, 0:T_own], func=AF.Exp,
                     scale=0.5, bias=mksc[:]).then_inc(sch.h, 1)
    P.s(s_mk6)
    t_mq = sch.inc(1)

    abf.reset()
    af32.reset()
    qg_sb = abf.carve(NCH * QG)
    kg = [abf.carve(NCH * KSC) for _ in range(2)]
    vg = [abf.carve((KSC // 128) * 1025) for _ in range(2)]
    p_buf = abf.carve(NCHU_F * QG)
    o_acc = af32.carve(NSUB * 1025)
    o_n = af32.carve(NSUB * D)
    xres = af32.carve(NSUB * D)

    spl_targets = [(splp[0].h, splp[0].n), (splp[1].h, splp[1].n),
                   (splv[0].h, splv[0].n), (splv[1].h, splv[1].n)]

    qg_free = 0
    pbuf_free = 0
    s1_stored = 0
    o_bank_free = {0: state.get("t_o0_free", 0), 1: 0}

    for g in range(n_groups):
        t_bankfree = bank_free[0]

        def t_mbc(e, g=g, t_bankfree=t_bankfree):
            e.wait_ge(sch.h, t_mq)
            wge(e, vch.h, t_bankfree)
            e.matmul(out=psA[:, 0:QG], lhsT=ones_row[:],
                     rhs=m_q[0:1, g * QG:(g + 1) * QG], start=True,
                     stop=True).then_inc(tch.h, 1)
        P.t(t_mbc)
        t_mbcmm = tch.inc(1)

        def v_mbc(e, t_mbcmm=t_mbcmm):
            e.wait_ge(tch.h, t_mbcmm)
            e.tensor_copy(out=m_bc[:], in_=psA[:, 0:QG]).then_inc(vch.h, 1)
        P.v(v_mbc)
        bank_free[0] = vch.inc(1)
        t_mbc_ready = bank_free[0]

        t_qgfree = qg_free
        srcq = qfm_d.ap().transpose([1, 0, 2])[:, :, g * QG:(g + 1) * QG]

        def sy_qg(e, srcq=srcq, t_qgfree=t_qgfree, g=g):
            if g == 0:
                for (h, n) in spl_targets:
                    wge(e, h, n)
                wge(e, tch.h, t_a1_tensor_end)
                wge(e, sch.h, t_mq)
            wge(e, tch.h, t_qgfree)
            e.dma_start(out=qg_sb.rearrange("p (c t) -> p c t", c=NCH),
                        in_=srcq).then_inc(ldqg.h, 16)
        P.sy(sy_qg)
        t_qg = ldqg.inc(16)

        for ki in range(n_ksc):
            par = ki % 2
            if ki >= 2:
                t_used = state.get(f"kvused_{g}_{ki - 2}", 0)
            else:
                t_used = state.get(f"kvused_{g - 1}_{n_ksc + ki - 2}", 0)
            srck = kfm_d.ap().transpose([1, 0, 2])[:, :, ki * KSC:(ki + 1) * KSC]
            srcv = vtm_d.ap()[ki * KSC:(ki + 1) * KSC, :].rearrange(
                "(j p) c -> p j c", p=128)

            def sy_kv(e, par=par, srck=srck, srcv=srcv, t_used=t_used, g=g, ki=ki):
                if g == 0 and ki <= 1:
                    for (h, n) in spl_targets:
                        wge(e, h, n)
                    wge(e, tch.h, t_a1_tensor_end)
                    wge(e, sch.h, t_mq)
                wge(e, tch.h, t_used)
                e.dma_start(out=kg[par].rearrange("p (c t) -> p c t", c=NCH),
                            in_=srck).then_inc(ldkv[par].h, 16)
                e.dma_start(out=vg[par].rearrange("p (j c) -> p j c", j=KSC // 128),
                            in_=srcv).then_inc(ldkv[par].h, 16)
            P.sy(sy_kv)
            t_kv = ldkv[par].inc(32)

            for kc in range(n_kc):
                chunk = ki * n_kc + kc
                bk = kc % 2
                t_bankfree = bank_free[bk]
                t_exp_prev = state.get(f"exp_{g}_{chunk - 2}", 0)

                def t_sc(e, g=g, par=par, ki=ki, kc=kc, bk=bk,
                         t_bankfree=t_bankfree, t_kv=t_kv, t_qg=t_qg):
                    e.wait_ge(ldkv[par].h, t_kv)
                    e.wait_ge(ldqg.h, t_qg)
                    wge(e, vch.h, t_bankfree)
                    for c in range(NCH):
                        ins = e.matmul(
                            out=psAB[bk][:, 0:QG],
                            lhsT=kg[par][:, c * KSC + kc * 128:c * KSC + (kc + 1) * 128],
                            rhs=qg_sb[:, c * QG:(c + 1) * QG],
                            start=(c == 0), stop=(c == NCH - 1))
                    ins.then_inc(tch.h, 1)
                P.t(t_sc)
                t_scmm = tch.inc(1)
                if ki == n_ksc - 1 and kc == n_kc - 1:
                    qg_free = t_scmm

                def v_stt(e, bk=bk, t_scmm=t_scmm, t_mb=t_mbc_ready,
                          t_exp_prev=t_exp_prev):
                    e.wait_ge(tch.h, t_scmm)
                    wge(e, vch.h, t_mb)
                    wge(e, sch.h, t_exp_prev)
                    e.scalar_tensor_tensor(out=tsb[bk][:], in0=psAB[bk][:, 0:QG],
                                           scalar=1.0, in1=m_bc[:],
                                           op0=ALU.mult, op1=ALU.subtract
                                           ).then_inc(vch.h, 1)
                P.v(v_stt)
                bank_free[bk] = vch.inc(1)
                t_stt = bank_free[bk]

                def s_exp(e, chunk=chunk, bk=bk, t_stt=t_stt, t_pb=pbuf_free):
                    e.wait_ge(vch.h, t_stt)
                    wge(e, tch.h, t_pb)
                    e.activation(out=p_buf[:, chunk * QG:(chunk + 1) * QG],
                                 in_=tsb[bk][:], func=AF.Exp).then_inc(sch.h, 1)
                P.s(s_exp)
                state[f"exp_{g}_{chunk}"] = sch.inc(1)
            t_exp_ksc = state[f"exp_{g}_{ki * n_kc + n_kc - 1}"]

            for sub in range(NSUB):
                om = sub % 2
                t_ofree = o_bank_free[om]

                def t_pv(e, g=g, ki=ki, sub=sub, om=om, par=par,
                         t_exp_ksc=t_exp_ksc, t_ofree=t_ofree):
                    e.wait_ge(sch.h, t_exp_ksc)
                    wge(e, vch.h, t_ofree)
                    for kc in range(n_kc):
                        chunk = ki * n_kc + kc
                        lhsT = p_buf[:, chunk * QG + sub * 128:chunk * QG + (sub + 1) * 128]
                        vb = vg[par]
                        e.matmul(out=psO[om][:, 0:512], lhsT=lhsT,
                                 rhs=vb[:, kc * 1025:kc * 1025 + 512],
                                 start=(kc == 0), stop=(kc == n_kc - 1))
                        e.matmul(out=psO[om][:, 512:1024], lhsT=lhsT,
                                 rhs=vb[:, kc * 1025 + 512:kc * 1025 + 1024],
                                 start=(kc == 0), stop=(kc == n_kc - 1))
                        ins = e.matmul(out=psO[om][:, 1024:1025], lhsT=lhsT,
                                       rhs=vb[:, kc * 1025 + 1024:kc * 1025 + 1025],
                                       start=(kc == 0), stop=(kc == n_kc - 1))
                    ins.then_inc(tch.h, 1)
                P.t(t_pv)
                t_pvmm = tch.inc(1)
                if sub == NSUB - 1:
                    state[f"kvused_{g}_{ki}"] = t_pvmm
                if ki == n_ksc - 1 and sub == NSUB - 1:
                    pbuf_free = t_pvmm

                def v_oacc(e, ki=ki, sub=sub, om=om, t_pvmm=t_pvmm,
                           t_s1st=s1_stored):
                    e.wait_ge(tch.h, t_pvmm)
                    if ki == 0:
                        wge(e, st.h, t_s1st)
                        e.tensor_copy(out=o_acc[:, sub * 1025:(sub + 1) * 1025],
                                      in_=psO[om][:, 0:1025]).then_inc(vch.h, 1)
                    else:
                        e.tensor_tensor(out=o_acc[:, sub * 1025:(sub + 1) * 1025],
                                        in0=o_acc[:, sub * 1025:(sub + 1) * 1025],
                                        in1=psO[om][:, 0:1025],
                                        op=ALU.add).then_inc(vch.h, 1)
                P.v(v_oacc)
                state[f"oevac_{g}_{ki}_{sub}"] = vch.inc(1)
                o_bank_free[om] = state[f"oevac_{g}_{ki}_{sub}"]
        t_oacc_all = state[f"oevac_{g}_{n_ksc - 1}_{NSUB - 1}"]

        def v_ep1(e, t_oacc_all=t_oacc_all):
            e.wait_ge(vch.h, t_oacc_all)
            for sub in range(NSUB):
                ins = e.reciprocal(out=rden[:, sub:sub + 1],
                                   in_=o_acc[:, sub * 1025 + 1024:sub * 1025 + 1025])
            ins.then_inc(vch.h, 1)
        P.v(v_ep1)
        t_ep1 = vch.inc(1)

        def v_ep2(e, t_ep1=t_ep1, t_s1st=s1_stored):
            e.wait_ge(vch.h, t_ep1)
            wge(e, st.h, t_s1st)
            for sub in range(NSUB):
                ins = e.tensor_scalar_mul(out=o_n[:, sub * D:(sub + 1) * D],
                                          in0=o_acc[:, sub * 1025:sub * 1025 + D],
                                          scalar1=rden[:, sub:sub + 1])
            ins.then_inc(vch.h, 1)
        P.v(v_ep2)
        t_ep2 = vch.inc(1)

        def v_pre_ep(e, t_ep2=t_ep2):
            e.wait_ge(vch.h, t_ep2)
        P.v(v_pre_ep)
        t_epap = emit_ln(P, lnb, vch, sch, nblk=NSUB,
                         src=o_n, srcol=lambda j: j * D,
                         dst=o_n, dstcol=lambda j: j * D)

        srcx = x_own[g * QG:(g + 1) * QG, :].rearrange("(j p) c -> p j c", p=128)

        def sy_xres(e, srcx=srcx, t_s1st=s1_stored):
            wge(e, st.h, t_s1st)
            e.dma_start(out=xres.rearrange("p (j c) -> p j c", j=NSUB),
                        in_=srcx).then_inc(ldxr.h, 16)
        P.sy(sy_xres)
        t_xres = ldxr.inc(16)

        def v_ep_res(e, t_epap=t_epap, t_xres=t_xres):
            e.wait_ge(sch.h, t_epap)
            e.wait_ge(ldxr.h, t_xres)
            e.tensor_tensor(out=o_n[:], in0=o_n[:], in1=xres[:],
                            op=ALU.add).then_inc(vch.h, 1)
        P.v(v_ep_res)
        t_eprs = vch.inc(1)

        dsts1 = s1[g * QG:(g + 1) * QG, :].rearrange("(j p) c -> p j c", p=128)

        def sy_s1(e, dsts1=dsts1, t_eprs=t_eprs):
            e.wait_ge(vch.h, t_eprs)
            e.dma_start(out=dsts1,
                        in_=o_n.rearrange("p (j c) -> p j c", j=NSUB)
                        ).then_inc(st.h, 16)
        P.sy(sy_s1)
        s1_stored = st.inc(16)

    def sy_fin(e, t=s1_stored):
        e.wait_ge(st.h, t)
    P.sy(sy_fin)

    block = es.enter_context(nc.Block())
    P.emit(block)
    es.close()
    return nc



def build_l0ffn(T_own, ST=256):
    H = 4096
    NHC = H // 128
    nst = T_own // ST
    NB = ST // 128
    W = ST + 2

    nc = bass.Bass("TRN2", target_bir_lowering=False)
    s1e = nc.dram_tensor("s1e", [T_own + 2, D], F32, kind="ExternalInput").ap()
    w1t = nc.dram_tensor("w1t", [D, H], F32, kind="ExternalInput").ap()
    cw = nc.dram_tensor("cw", [H, 3], F32, kind="ExternalInput").ap()
    cb = nc.dram_tensor("cb", [H], F32, kind="ExternalInput").ap()
    w2t = nc.dram_tensor("w2t", [H, D], F32, kind="ExternalInput").ap()
    s2 = nc.dram_tensor("s2", [T_own, D], F32, kind="ExternalOutput").ap()

    es = contextlib.ExitStack()
    P = Prog(nc)

    vch = Ctr(es, nc, "vch")
    sch = Ctr(es, nc, "sch")
    tch = Ctr(es, nc, "tch")
    gch = Ctr(es, nc, "gch")
    ldw = [Ctr(es, nc, f"ldw{p}") for p in range(2)]
    ldx = [Ctr(es, nc, f"ldx{p}") for p in range(2)]
    ldxr = Ctr(es, nc, "ldxr")
    ldc = Ctr(es, nc, "ldc")
    st_c = Ctr(es, nc, "st")

    consts = build_consts(nc, es, P, gch)
    ident_f32 = consts["ident_f32"]
    ones_row = consts["ones_row"]
    t_consts = consts["ready"]

    abf = Arena(nc, es, "abf", BF16, 77000)
    af32 = Arena(nc, es, "af32", F32, 8800)

    wq1_bf = abf.carve(NCH * H)
    wq2_bf = abf.carve(NHC * D)
    hbig = abf.carve(NHC * ST)
    xqfm = abf.carve(NCH * W)
    hsb = [abf.carve(W) for _ in range(2)]

    stg = [af32.carve(H) for _ in range(2)]
    af32.reset()
    xwin = af32.carve((NB + 1) * D)
    cacc = [af32.carve(ST) for _ in range(2)]
    tmp32 = af32.carve(ST)
    sc2b = af32.carve(ST)
    out2 = af32.carve(NB * D)
    xres = af32.carve(NB * D)

    lnb = mk_ln_bufs(nc, es, NB + 1)
    cwsb = es.enter_context(nc.sbuf_tensor("cwsb", [128, NHC * 3], F32))
    cbsb = es.enter_context(nc.sbuf_tensor("cbsb", [128, NHC], F32))
    ones_col32 = es.enter_context(nc.sbuf_tensor("ones_col32", [128, 1], F32))
    ones128 = es.enter_context(nc.sbuf_tensor("ones128", [128, 128], F32))
    colsum = es.enter_context(nc.sbuf_tensor("colsum", [128, 1], F32))
    asum = es.enter_context(nc.sbuf_tensor("asum", [128, 1], F32))
    wsum = es.enter_context(nc.sbuf_tensor("wsum", [1, 1], F32))
    meanc = [es.enter_context(nc.sbuf_tensor(f"meanc{i}", [1, 1], F32)) for i in range(2)]
    meanb = [es.enter_context(nc.sbuf_tensor(f"meanb{i}", [128, 1], F32)) for i in range(2)]
    wsb = es.enter_context(nc.sbuf_tensor("wsb", [128, 1], F32))
    amaxb = es.enter_context(nc.sbuf_tensor("amaxb", [128, NB + 1], F32))
    scb = es.enter_context(nc.sbuf_tensor("scb", [128, NB + 1], F32))
    fb = es.enter_context(nc.sbuf_tensor("fb", [128, NB + 1], F32))
    maxacc = es.enter_context(nc.sbuf_tensor("maxacc", [128, ST], F32))
    tmax = es.enter_context(nc.sbuf_tensor("tmax", [128, NB], F32))
    sc2_tm = es.enter_context(nc.sbuf_tensor("sc2_tm", [128, NB], F32))
    g_tm = es.enter_context(nc.sbuf_tensor("g_tm", [128, NB], F32))
    rep = es.enter_context(nc.sbuf_tensor("rep", [128, 128], F32))

    psA = es.enter_context(nc.psum_tensor("psA", [128, 512], F32))
    psB = es.enter_context(nc.psum_tensor("psB", [128, 512], F32))
    psH = [es.enter_context(nc.psum_tensor(f"psH{p}", [128, W], F32)) for p in range(2)]
    psM = es.enter_context(nc.psum_tensor("psM", [128, 1024], F32))
    psAB = [psA, psB]

    def v_init(e):
        e.memset(ones_col32[:], 1.0)
        e.memset(lnb["veps"][:], 1.0)
        e.memset(amaxb[:], 1.0)
        e.memset(scb[:], 1.0)
        e.memset(fb[:], 1.0)
        e.memset(ones128[:], 1.0).then_inc(vch.h, 1)
    P.v(v_init)
    t_vinit = vch.inc(1)

    cbstg = es.enter_context(nc.sbuf_tensor("cbstg", [32, 128], F32))

    def sy_cw(e):
        e.dma_start(out=cwsb[:].rearrange("p (n m) -> p n m", n=NHC),
                    in_=cw.rearrange("(n p) m -> p n m", p=128)).then_inc(ldc.h, 16)
        e.dma_start(out=cbstg[:], in_=cb.rearrange("(n p) -> n p", p=128)
                    ).then_inc(ldc.h, 16)
    P.sy(sy_cw)
    t_ldc0 = ldc.inc(32)

    def t_cb(e):
        e.wait_ge(ldc.h, t_ldc0)
        wge(e, gch.h, t_consts)
        e.transpose(out=psB[:, 0:32], in_=cbstg[:],
                    identity=ident_f32[0:32, 0:32]).then_inc(tch.h, 1)
    P.t(t_cb)
    t_cbt = tch.inc(1)

    def v_cb(e):
        e.wait_ge(tch.h, t_cbt)
        e.tensor_copy(out=cbsb[:], in_=psB[:, 0:32]).then_inc(vch.h, 1)
    P.v(v_cb)
    t_cb_ev = vch.inc(1)
    t_ldc = t_ldc0

    state = {}

    def emit_wquant(widx, src_rows_ap, n_chunks, ccols, wq_dst, dstcol,
                    t_prev_done=0):
        numel = n_chunks * 128 * ccols
        for ch in range(n_chunks):
            par = ch % 2
            t_used = state.get(f"wq{widx}_used_{ch - 2}", t_prev_done)
            if ch < 2:
                t_used = max(t_used, t_prev_done)
            src = src_rows_ap[ch * 128:(ch + 1) * 128, :]

            def sy_ld(e, par=par, src=src, ccols=ccols, t_used=t_used):
                wge(e, vch.h, t_used)
                e.dma_start(out=stg[par][:, 0:ccols], in_=src).then_inc(ldw[par].h, 16)
            P.sy(sy_ld)
            t_ld = ldw[par].inc(16)

            t_prev_acc = state.get(f"wq{widx}_used_{ch - 1}", 0)

            def v_red(e, par=par, ccols=ccols, t_ld=t_ld, ch=ch,
                      t_prev_acc=t_prev_acc):
                e.wait_ge(ldw[par].h, t_ld)
                wge(e, vch.h, t_prev_acc)
                e.tensor_reduce(out=asum[:], in_=stg[par][:, 0:ccols], axis=AX.X,
                                op=ALU.add, apply_absolute_value=True
                                ).then_inc(vch.h, 1)
            P.v(v_red)
            t_red = vch.inc(1)

            def v_accum(e, ch=ch, t_red=t_red):
                e.wait_ge(vch.h, t_red)
                if ch == 0:
                    e.tensor_copy(out=colsum[:], in_=asum[:]).then_inc(vch.h, 1)
                else:
                    e.tensor_tensor(out=colsum[:], in0=colsum[:], in1=asum[:],
                                    op=ALU.add).then_inc(vch.h, 1)
            P.v(v_accum)
            state[f"wq{widx}_used_{ch}"] = vch.inc(1)
        t_sum = state[f"wq{widx}_used_{n_chunks - 1}"]

        def t_psum(e, t_sum=t_sum):
            e.wait_ge(vch.h, max(t_sum, t_vinit))
            e.matmul(out=psA[0:1, 0:1], lhsT=colsum[:], rhs=ones_col32[:],
                     start=True, stop=True).then_inc(tch.h, 1)
        P.t(t_psum)
        t_ps = tch.inc(1)

        def v_mean(e, t_ps=t_ps, numel=numel, widx=widx):
            e.wait_ge(tch.h, t_ps)
            e.tensor_scalar(out=meanc[widx][:], in0=psA[0:1, 0:1],
                            scalar1=1.0 / numel, scalar2=1e-5,
                            op0=ALU.mult, op1=ALU.max).then_inc(vch.h, 1)
        P.v(v_mean)
        t_mean = vch.inc(1)

        def v_ws(e, t_mean=t_mean, widx=widx):
            e.wait_ge(vch.h, t_mean)
            e.reciprocal(out=wsum[:], in_=meanc[widx][:]).then_inc(vch.h, 1)
        P.v(v_ws)
        t_wsv = vch.inc(1)

        def t_bc(e, t_wsv=t_wsv, widx=widx):
            e.wait_ge(vch.h, t_wsv)
            wge(e, gch.h, t_consts)
            e.matmul(out=psA[:, 1:2], lhsT=ones_row[:], rhs=wsum[:],
                     start=True, stop=True)
            e.matmul(out=psA[:, 2:3], lhsT=ones_row[:], rhs=meanc[widx][:],
                     start=True, stop=True).then_inc(tch.h, 1)
        P.t(t_bc)
        t_bcm = tch.inc(1)

        def v_bc(e, t_bcm=t_bcm, widx=widx):
            e.wait_ge(tch.h, t_bcm)
            e.tensor_copy(out=wsb[:], in_=psA[:, 1:2])
            e.tensor_copy(out=meanb[widx][:], in_=psA[:, 2:3]).then_inc(vch.h, 1)
        P.v(v_bc)
        t_wsb = vch.inc(1)

        for ch in range(n_chunks):
            par = ch % 2
            t_used = state.get(f"wq{widx}b_used_{ch - 2}", 0)
            src = src_rows_ap[ch * 128:(ch + 1) * 128, :]

            def sy_ld2(e, par=par, src=src, ccols=ccols, t_used=t_used,
                       t_wsb=t_wsb, ch=ch):
                if ch <= 1:
                    wge(e, vch.h, t_wsb)
                wge(e, vch.h, t_used)
                e.dma_start(out=stg[par][:, 0:ccols], in_=src).then_inc(ldw[par].h, 16)
            P.sy(sy_ld2)
            t_ld = ldw[par].inc(16)

            def v_q1(e, par=par, ccols=ccols, t_ld=t_ld, t_wsb=t_wsb):
                e.wait_ge(ldw[par].h, t_ld)
                wge(e, vch.h, t_wsb)
                e.tensor_scalar(out=stg[par][:, 0:ccols], in0=stg[par][:, 0:ccols],
                                scalar1=wsb[:], scalar2=MAGIC,
                                op0=ALU.mult, op1=ALU.add).then_inc(vch.h, 1)
            P.v(v_q1)
            t_q1 = vch.inc(1)

            def v_q2(e, par=par, ccols=ccols, t_q1=t_q1):
                e.wait_ge(vch.h, t_q1)
                e.tensor_scalar(out=stg[par][:, 0:ccols], in0=stg[par][:, 0:ccols],
                                scalar1=MAGIC, scalar2=1.0,
                                op0=ALU.subtract, op1=ALU.min).then_inc(vch.h, 1)
            P.v(v_q2)
            t_q2 = vch.inc(1)

            def v_q3(e, par=par, ccols=ccols, t_q2=t_q2, ch=ch):
                e.wait_ge(vch.h, t_q2)
                e.tensor_scalar_max(out=wq_dst[:, dstcol(ch):dstcol(ch) + ccols],
                                    in0=stg[par][:, 0:ccols],
                                    scalar1=-1.0).then_inc(vch.h, 1)
            P.v(v_q3)
            state[f"wq{widx}b_used_{ch}"] = vch.inc(1)
        return state[f"wq{widx}b_used_{n_chunks - 1}"]

    t_wq1 = emit_wquant(0, w1t, NCH, H, wq1_bf, lambda ch: ch * H)
    t_wq2 = emit_wquant(1, w2t, NHC, D, wq2_bf, lambda ch: ch * D,
                        t_prev_done=t_wq1)

    bank_free = {0: 0, 1: t_cb_ev}
    psh_free = {0: 0, 1: 0}
    psm_free = 0
    apply_done = {}
    s2_stored = 0

    for si in range(nst):
        par = si % 2
        t0 = si * ST
        blocks = [(j * D, j * D, 128) for j in range(NB)] + [(NB * D, NB * D, 2)]
        t_prev_tp = state.get(f"btp_{si - 1}", 0)
        t_prev_apply = apply_done.get(si - 1, 0)

        def sy_win(e, t0=t0, t_prev_tp=t_prev_tp, si=si):
            if si == 0:
                wge(e, vch.h, t_wq2)
            wge(e, tch.h, t_prev_tp)
            e.dma_start(out=xwin.rearrange("p (j c) -> p j c", j=NB + 1)[:, 0:NB, :],
                        in_=s1e[t0:t0 + NB * 128, :].rearrange("(j p) c -> p j c", p=128)
                        ).then_inc(ldx[0].h, 16)
            e.dma_start(out=xwin[0:2, NB * D:(NB + 1) * D],
                        in_=s1e[t0 + NB * 128:t0 + W, :]).then_inc(ldx[0].h, 16)
        P.sy(sy_win)
        t_win = ldx[0].inc(32)

        def v_predma(e, t_win=t_win):
            e.wait_ge(ldx[0].h, t_win)
        P.v(v_predma)
        t_apply = emit_ln_blocks(P, lnb, vch, sch, src=xwin, dst=xwin,
                                 blocks=blocks, war_s=(tch.h, t_prev_tp))
        apply_done[si] = t_apply

        def v_amax(e, blocks=blocks, t_apply=t_apply):
            e.wait_ge(sch.h, t_apply)
            for j, (sc, dc, r) in enumerate(blocks):
                ins = e.tensor_reduce(out=amaxb[0:r, j:j + 1],
                                      in_=xwin[0:r, sc:sc + D], axis=AX.X,
                                      op=ALU.max, apply_absolute_value=True)
            ins.then_inc(vch.h, 1)
        P.v(v_amax)
        t_am = vch.inc(1)

        def v_clip(e, blocks=blocks, t_am=t_am):
            e.wait_ge(vch.h, t_am)
            nb1 = len(blocks)
            ins = e.tensor_scalar_max(out=amaxb[:, 0:nb1], in0=amaxb[:, 0:nb1],
                                      scalar1=1e-5)
            ins.then_inc(vch.h, 1)
        P.v(v_clip)
        t_cl = vch.inc(1)

        def v_rsc(e, blocks=blocks, t_cl=t_cl):
            e.wait_ge(vch.h, t_cl)
            nb1 = len(blocks)
            e.reciprocal(out=scb[:, 0:nb1], in_=amaxb[:, 0:nb1]).then_inc(vch.h, 1)
        P.v(v_rsc)
        t_rs = vch.inc(1)

        def v_sc(e, blocks=blocks, t_rs=t_rs, t_w=t_wq1):
            e.wait_ge(vch.h, max(t_rs, t_w))
            nb1 = len(blocks)
            e.tensor_scalar_mul(out=scb[:, 0:nb1], in0=scb[:, 0:nb1], scalar1=127.0)
            e.tensor_scalar_mul(out=fb[:, 0:nb1], in0=amaxb[:, 0:nb1],
                                scalar1=1.0 / 127.0).then_inc(vch.h, 1)
        P.v(v_sc)
        t_sc = vch.inc(1)

        def v_f(e, blocks=blocks, t_sc=t_sc):
            e.wait_ge(vch.h, t_sc)
            nb1 = len(blocks)
            for j in range(nb1):
                ins = e.tensor_tensor(out=fb[:, j:j + 1], in0=fb[:, j:j + 1],
                                      in1=meanb[0][:], op=ALU.mult)
            ins.then_inc(vch.h, 1)
        P.v(v_f)
        t_f = vch.inc(1)

        def v_bq1(e, blocks=blocks, t_f=t_f):
            e.wait_ge(vch.h, t_f)
            for j, (sc, dc, r) in enumerate(blocks):
                ins = e.tensor_scalar(out=xwin[0:r, sc:sc + D],
                                      in0=xwin[0:r, sc:sc + D],
                                      scalar1=scb[0:r, j:j + 1], scalar2=MAGIC,
                                      op0=ALU.mult, op1=ALU.add)
            ins.then_inc(vch.h, 1)
        P.v(v_bq1)
        t_b1 = vch.inc(1)

        def v_bq2(e, blocks=blocks, t_b1=t_b1):
            e.wait_ge(vch.h, t_b1)
            for j, (sc, dc, r) in enumerate(blocks):
                ins = e.tensor_scalar(out=xwin[0:r, sc:sc + D],
                                      in0=xwin[0:r, sc:sc + D],
                                      scalar1=MAGIC, scalar2=127.0,
                                      op0=ALU.subtract, op1=ALU.min)
            ins.then_inc(vch.h, 1)
        P.v(v_bq2)
        t_b2 = vch.inc(1)

        def v_bq3(e, blocks=blocks, t_b2=t_b2):
            e.wait_ge(vch.h, t_b2)
            for j, (sc, dc, r) in enumerate(blocks):
                ins = e.tensor_scalar(out=xwin[0:r, sc:sc + D],
                                      in0=xwin[0:r, sc:sc + D],
                                      scalar1=-128.0, scalar2=fb[0:r, j:j + 1],
                                      op0=ALU.max, op1=ALU.mult)
            ins.then_inc(vch.h, 1)
        P.v(v_bq3)
        t_b3 = vch.inc(1)

        n_tp = 0
        for j in range(NB + 1):
            r = 128 if j < NB else 2
            for h in range(2):
                bk = n_tp % 2
                t_bankfree = bank_free[bk]

                def t_tp(e, j=j, h=h, r=r, bk=bk, t_bankfree=t_bankfree,
                         t_b3=t_b3):
                    e.wait_ge(vch.h, max(t_b3, t_bankfree))
                    wge(e, gch.h, t_consts)
                    for cc in range(4):
                        c = 4 * h + cc
                        ins = e.transpose(
                            out=psAB[bk][0:r if False else slice(None), cc * 128:cc * 128 + r] if False else psAB[bk][:, cc * 128:cc * 128 + r],
                            in_=xwin[0:r, j * D + c * 128:j * D + (c + 1) * 128],
                            identity=ident_f32[0:r, 0:r])
                    ins.then_inc(tch.h, 1)
                P.t(t_tp)
                t_tpg = tch.inc(1)

                def v_evtp(e, j=j, h=h, r=r, bk=bk, t_tpg=t_tpg):
                    e.wait_ge(tch.h, t_tpg)
                    dst = xqfm.rearrange("p (c t) -> p c t", c=NCH)
                    e.tensor_copy(out=dst[:, 4 * h:4 * h + 4, j * 128:j * 128 + r],
                                  in_=psAB[bk][:].rearrange("p (c t) -> p c t", c=4)[:, :, 0:r]
                                  ).then_inc(vch.h, 1)
                P.v(v_evtp)
                bank_free[bk] = vch.inc(1)
                n_tp += 1
        state[f"btp_{si}"] = tch.n
        t_xq_ready = vch.n

        t_tpm_prev = state.get(f"tpm_{si - 1}", 0)

        def v_mx0(e, t_tpm_prev=t_tpm_prev):
            wge(e, tch.h, t_tpm_prev)
            e.memset(maxacc[:, 0:ST], 0.0).then_inc(vch.h, 1)
        P.v(v_mx0)
        t_mx0 = vch.inc(1)

        for hc in range(NHC):
            php = hc % 2
            t_phfree = psh_free[php]

            assert W * 4 <= 2048

            def t_mm1(e, hc=hc, php=php, t_phfree=t_phfree, t_xq=t_xq_ready,
                      t_w=t_wq1):
                e.wait_ge(vch.h, max(t_xq, t_w, t_phfree))
                for c in range(NCH):
                    e.matmul(out=psH[php][:, 0:ST],
                             lhsT=wq1_bf[:, c * H + hc * 128:c * H + (hc + 1) * 128],
                             rhs=xqfm[:, c * W:c * W + ST],
                             start=(c == 0), stop=False)
                    ins = e.matmul(out=psH[php][:, ST:W],
                                   lhsT=wq1_bf[:, c * H + hc * 128:c * H + (hc + 1) * 128],
                                   rhs=xqfm[:, c * W + ST:c * W + W],
                                   start=False, stop=(c == NCH - 1))
                ins.then_inc(tch.h, 1)
            P.t(t_mm1)
            t_m1 = tch.inc(1)

            t_conv_prev = state.get(f"conv3_{hc - 2}", 0)

            def v_ehs(e, hc=hc, php=php, t_m1=t_m1, t_conv_prev=t_conv_prev):
                e.wait_ge(tch.h, t_m1)
                wge(e, vch.h, t_conv_prev)
                e.tensor_copy(out=hsb[php], in_=psH[php][:, 0:W]).then_inc(vch.h, 1)
            P.v(v_ehs)
            psh_free[php] = 0
            t_eh = vch.inc(1)
            psh_free[php] = t_eh

            t_cacc_free = state.get(f"conv3_{hc - 2}", 0)

            def s_c1(e, hc=hc, php=php, t_eh=t_eh, t_ldc=t_ldc,
                     t_cacc_free=t_cacc_free):
                e.wait_ge(vch.h, max(t_eh, t_cacc_free))
                e.wait_ge(ldc.h, t_ldc)
                e.activation(out=cacc[php][:, 0:ST], in_=hsb[php][:, 0:ST],
                             func=AF.Identity,
                             scale=cwsb[:, hc * 3:hc * 3 + 1]).then_inc(sch.h, 1)
            P.s(s_c1)
            t_c1 = sch.inc(1)

            def v_c2(e, hc=hc, php=php, t_c1=t_c1):
                e.wait_ge(sch.h, t_c1)
                e.scalar_tensor_tensor(out=cacc[php][:, 0:ST],
                                       in0=hsb[php][:, 1:1 + ST],
                                       scalar=cwsb[:, hc * 3 + 1:hc * 3 + 2],
                                       in1=cacc[php][:, 0:ST],
                                       op0=ALU.mult, op1=ALU.add).then_inc(vch.h, 1)
            P.v(v_c2)
            state[f"conv1_{hc}"] = vch.inc(1)
            t_c2 = state[f"conv1_{hc}"]

            t_tmp_free = state.get(f"silu_{hc - 1}", 0)

            def v_c3(e, hc=hc, php=php, t_c2=t_c2, t_tmp_free=t_tmp_free):
                e.wait_ge(vch.h, t_c2)
                wge(e, sch.h, t_tmp_free)
                e.scalar_tensor_tensor(out=tmp32[:, 0:ST],
                                       in0=hsb[php][:, 2:2 + ST],
                                       scalar=cwsb[:, hc * 3 + 2:hc * 3 + 3],
                                       in1=cacc[php][:, 0:ST],
                                       op0=ALU.mult, op1=ALU.add).then_inc(vch.h, 1)
            P.v(v_c3)
            state[f"conv3_{hc}"] = vch.inc(1)
            t_c3 = state[f"conv3_{hc}"]

            t_q2done = state.get(f"q2_{hc}", None)
            t_q2done = state.get(f"bq2_{si - 1}_last", 0)

            def s_silu(e, hc=hc, t_c3=t_c3, t_q2done=t_q2done):
                e.wait_ge(vch.h, max(t_c3, t_cb_ev))
                wge(e, tch.h, t_q2done)
                e.activation(out=hbig[:, hc * ST:(hc + 1) * ST], in_=tmp32[:, 0:ST],
                             func=AF.Silu, bias=cbsb[:, hc:hc + 1]).then_inc(sch.h, 1)
            P.s(s_silu)
            state[f"silu_{hc}"] = sch.inc(1)
            t_sl = state[f"silu_{hc}"]

            def v_mx(e, hc=hc, t_sl=t_sl, t_mx0=t_mx0):
                e.wait_ge(sch.h, t_sl)
                e.wait_ge(vch.h, t_mx0)
                e.tensor_tensor(out=maxacc[:, 0:ST], in0=maxacc[:, 0:ST],
                                in1=hbig[:, hc * ST:(hc + 1) * ST],
                                op=ALU.max).then_inc(vch.h, 1)
            P.v(v_mx)
            t_mxa = vch.inc(1)

            def v_mx2(e, hc=hc, t_mxa=t_mxa):
                e.wait_ge(vch.h, t_mxa)
                e.scalar_tensor_tensor(out=maxacc[:, 0:ST],
                                       in0=hbig[:, hc * ST:(hc + 1) * ST],
                                       scalar=-1.0, in1=maxacc[:, 0:ST],
                                       op0=ALU.mult, op1=ALU.max).then_inc(vch.h, 1)
            P.v(v_mx2)
            state[f"mx_{hc}"] = vch.inc(1)
        t_mx_all = state[f"mx_{NHC - 1}"]

        for j in range(NB):
            bk = j % 2
            t_bankfree = bank_free[bk]

            def t_tpm(e, j=j, bk=bk, t_mx_all=t_mx_all, t_bankfree=t_bankfree):
                e.wait_ge(vch.h, max(t_mx_all, t_bankfree))
                e.transpose(out=psAB[bk][:, 0:128],
                            in_=maxacc[:, j * 128:(j + 1) * 128],
                            identity=ident_f32[:]).then_inc(tch.h, 1)
            P.t(t_tpm)
            t_tj = tch.inc(1)

            def v_redm(e, j=j, bk=bk, t_tj=t_tj):
                e.wait_ge(tch.h, t_tj)
                e.tensor_reduce(out=tmax[:, j:j + 1], in_=psAB[bk][:, 0:128],
                                axis=AX.X, op=ALU.max).then_inc(vch.h, 1)
            P.v(v_redm)
            bank_free[bk] = vch.inc(1)
        t_tmax = bank_free[(NB - 1) % 2]
        state[f"tpm_{si}"] = tch.n

        def v_sc2(e, t_tmax=t_tmax, t_w2=t_wq2):
            e.wait_ge(vch.h, max(t_tmax, t_w2))
            e.tensor_scalar_max(out=tmax[:, 0:NB], in0=tmax[:, 0:NB],
                                scalar1=1e-5).then_inc(vch.h, 1)
        P.v(v_sc2)
        t_s2pre = vch.inc(1)

        def v_sc2r(e, t_s2pre=t_s2pre):
            e.wait_ge(vch.h, t_s2pre)
            e.reciprocal(out=sc2_tm[:, 0:NB], in_=tmax[:, 0:NB]).then_inc(vch.h, 1)
        P.v(v_sc2r)
        t_s2a = vch.inc(1)

        def v_sc2b(e, t_s2a=t_s2a):
            e.wait_ge(vch.h, t_s2a)
            e.tensor_scalar_mul(out=sc2_tm[:, 0:NB], in0=sc2_tm[:, 0:NB],
                                scalar1=127.0)
            e.tensor_scalar_mul(out=g_tm[:, 0:NB], in0=tmax[:, 0:NB],
                                scalar1=1.0 / 127.0).then_inc(vch.h, 1)
        P.v(v_sc2b)
        t_s2b = vch.inc(1)

        def v_g(e, t_s2b=t_s2b):
            e.wait_ge(vch.h, t_s2b)
            for j in range(NB):
                ins = e.tensor_tensor(out=g_tm[:, j:j + 1], in0=g_tm[:, j:j + 1],
                                      in1=meanb[1][:], op=ALU.mult)
            ins.then_inc(vch.h, 1)
        P.v(v_g)
        t_g = vch.inc(1)

        for j in range(NB):
            bk = j % 2
            t_bankfree = bank_free[bk]

            def v_rep(e, j=j, t_s2b=t_s2b, t_vinit=t_vinit):
                e.wait_ge(vch.h, max(t_s2b, t_vinit))
                e.tensor_scalar_mul(out=rep[:], in0=ones128[:],
                                    scalar1=sc2_tm[:, j:j + 1]).then_inc(vch.h, 1)
            P.v(v_rep)
            t_rep = vch.inc(1)

            def t_tpr(e, j=j, bk=bk, t_rep=t_rep, t_bankfree=t_bankfree):
                e.wait_ge(vch.h, max(t_rep, t_bankfree))
                e.transpose(out=psAB[bk][:, 0:128], in_=rep[:],
                            identity=ident_f32[:]).then_inc(tch.h, 1)
            P.t(t_tpr)
            t_tpj = tch.inc(1)

            def v_esc(e, j=j, bk=bk, t_tpj=t_tpj):
                e.wait_ge(tch.h, t_tpj)
                e.tensor_copy(out=sc2b[:, j * 128:(j + 1) * 128],
                              in_=psAB[bk][:, 0:128]).then_inc(vch.h, 1)
            P.v(v_esc)
            bank_free[bk] = vch.inc(1)
        t_sc2b_ready = bank_free[(NB - 1) % 2]

        for hc in range(NHC):
            t_qh_prev = state.get(f"qh_{hc - 1}", 0)

            def v_qh1(e, hc=hc, t_sc2b_ready=t_sc2b_ready, t_qh_prev=t_qh_prev):
                e.wait_ge(vch.h, max(t_sc2b_ready, t_qh_prev))
                e.tensor_tensor(out=tmp32[:, 0:ST], in0=hbig[:, hc * ST:(hc + 1) * ST],
                                in1=sc2b[:, 0:ST], op=ALU.mult).then_inc(vch.h, 1)
            P.v(v_qh1)
            t_h1 = vch.inc(1)

            def v_qh2(e, hc=hc, t_h1=t_h1):
                e.wait_ge(vch.h, t_h1)
                e.tensor_scalar(out=tmp32[:, 0:ST], in0=tmp32[:, 0:ST],
                                scalar1=MAGIC, scalar2=MAGIC,
                                op0=ALU.add, op1=ALU.subtract).then_inc(vch.h, 1)
            P.v(v_qh2)
            t_h2 = vch.inc(1)

            def v_qh3(e, hc=hc, t_h2=t_h2):
                e.wait_ge(vch.h, t_h2)
                e.tensor_scalar(out=hbig[:, hc * ST:(hc + 1) * ST],
                                in0=tmp32[:, 0:ST],
                                scalar1=127.0, scalar2=-128.0,
                                op0=ALU.min, op1=ALU.max).then_inc(vch.h, 1)
            P.v(v_qh3)
            state[f"qh_{hc}"] = vch.inc(1)
        t_q_all = state[f"qh_{NHC - 1}"]

        for j in range(NB):
            def t_mm2(e, j=j, t_q_all=t_q_all, t_w2=t_wq2, t_psm=psm_free):
                e.wait_ge(vch.h, max(t_q_all, t_w2, t_psm))
                for hc in range(NHC):
                    e.matmul(out=psM[:, 0:512],
                             lhsT=hbig[:, hc * ST + j * 128:hc * ST + (j + 1) * 128],
                             rhs=wq2_bf[:, hc * D:hc * D + 512],
                             start=(hc == 0), stop=(hc == NHC - 1))
                    ins = e.matmul(out=psM[:, 512:1024],
                                   lhsT=hbig[:, hc * ST + j * 128:hc * ST + (j + 1) * 128],
                                   rhs=wq2_bf[:, hc * D + 512:hc * D + 1024],
                                   start=(hc == 0), stop=(hc == NHC - 1))
                ins.then_inc(tch.h, 1)
            P.t(t_mm2)
            t_m2 = tch.inc(1)
            if j == NB - 1:
                state[f"bq2_{si}_last"] = t_m2

            def v_em2(e, j=j, t_m2=t_m2, t_g=t_g, t_s2st=s2_stored):
                e.wait_ge(tch.h, t_m2)
                wge(e, vch.h, t_g)
                wge(e, st_c.h, t_s2st)
                e.tensor_scalar_mul(out=out2[:, j * D:(j + 1) * D], in0=psM[:, 0:1024],
                                    scalar1=g_tm[:, j:j + 1]).then_inc(vch.h, 1)
            P.v(v_em2)
            psm_free = vch.inc(1)
        t_out2 = psm_free

        def v_pre(e, t_out2=t_out2):
            e.wait_ge(vch.h, t_out2)
        P.v(v_pre)
        t_ap2 = emit_ln(P, lnb, vch, sch, nblk=NB,
                        src=out2, srcol=lambda j: j * D,
                        dst=out2, dstcol=lambda j: j * D)

        def sy_xres(e, t0=t0, t_s2st=s2_stored):
            wge(e, st_c.h, t_s2st)
            e.dma_start(out=xres.rearrange("p (j c) -> p j c", j=NB),
                        in_=s1e[t0 + 1:t0 + 1 + ST, :].rearrange("(j p) c -> p j c", p=128)
                        ).then_inc(ldxr.h, 16)
        P.sy(sy_xres)
        t_xr = ldxr.inc(16)

        def v_res(e, t_ap2=t_ap2, t_xr=t_xr, t_s2st=s2_stored):
            e.wait_ge(sch.h, t_ap2)
            e.wait_ge(ldxr.h, t_xr)
            wge(e, st_c.h, t_s2st)
            e.tensor_tensor(out=out2[:, 0:NB * D], in0=out2[:, 0:NB * D],
                            in1=xres[:, 0:NB * D], op=ALU.add).then_inc(vch.h, 1)
        P.v(v_res)
        t_resv = vch.inc(1)

        def sy_s2(e, t0=t0, t_resv=t_resv):
            e.wait_ge(vch.h, t_resv)
            e.dma_start(out=s2[t0:t0 + ST, :].rearrange("(j p) c -> p j c", p=128),
                        in_=out2.rearrange("p (j c) -> p j c", j=NB)
                        ).then_inc(st_c.h, 16)
        P.sy(sy_s2)
        s2_stored = st_c.inc(16)

    def sy_fin(e, t=s2_stored):
        e.wait_ge(st_c.h, t)
    P.sy(sy_fin)

    block = es.enter_context(nc.Block())
    P.emit(block)
    es.close()
    return nc



def l1_proj_tiles(ctx, si, t0, *, do_q):
    (nc, P, es, state, vch, sch, tch, gch) = (ctx[k] for k in
        ("nc", "P", "es", "state", "vch", "sch", "tch", "gch"))
    psAB = ctx["psAB"]
    bank_free = ctx["bank_free"]
    ident_f32 = ctx["ident_f32"]
    ident_bf = ctx["ident_bf"]
    t_consts = ctx["t_consts"]
    lnb = ctx["lnb"]
    ldx = ctx["ldx"]
    s2src = ctx["s2src"]
    xst = ctx["xst"]
    xfm = ctx["xfm"]
    TILE = 512
    NBLK = 4
    nbuf = ctx["nbuf"]

    p = si % nbuf
    t_apfree = state.get(f"l1ap_{si - nbuf}", 0)
    t_tp_prev = state.get(f"l1tp_{si - nbuf}", 0)
    src = s2src[t0:t0 + TILE, :].rearrange("(j p) c -> p j c", p=128)

    def sy_load(e, src=src, p=p, t_apfree=t_apfree, t_tp_prev=t_tp_prev, si=si):
        if si < nbuf:
            wge(e, vch.h, ctx.get("t_wdone", 0))
        wge(e, sch.h, t_apfree)
        wge(e, tch.h, t_tp_prev)
        e.dma_start(out=xst[p].rearrange("p (j c) -> p j c", j=NBLK),
                    in_=src).then_inc(ldx[p].h, 16)
    P.sy(sy_load)
    t_ldx = ldx[p].inc(16)

    def v_predma(e, p=p, t_ldx=t_ldx):
        e.wait_ge(ldx[p].h, t_ldx)
    P.v(v_predma)
    t_apply = emit_ln(P, lnb, vch, sch, nblk=NBLK,
                      src=xst[p], srcol=lambda j: j * D,
                      dst=xst[p], dstcol=lambda j: j * D,
                      war_s=(tch.h, t_tp_prev))
    state[f"l1ap_{si}"] = t_apply

    for j in range(NBLK):
        for h in range(2):
            bk = (j * 2 + h) % 2
            t_bankfree = bank_free[bk]

            def t_tp(e, p=p, j=j, h=h, bk=bk, t_bankfree=t_bankfree,
                     t_ap=t_apply):
                e.wait_ge(sch.h, t_ap)
                wge(e, gch.h, t_consts)
                wge(e, vch.h, t_bankfree)
                for cc in range(4):
                    c = 4 * h + cc
                    ins = e.transpose(
                        out=psAB[bk][:, cc * 128:(cc + 1) * 128],
                        in_=xst[p][:, j * D + c * 128:j * D + (c + 1) * 128],
                        identity=ident_f32[:])
                ins.then_inc(tch.h, 1)
            P.t(t_tp)
            t_tpg = tch.inc(1)

            def v_evac_tp(e, p=p, j=j, h=h, bk=bk, t_tpg=t_tpg):
                e.wait_ge(tch.h, t_tpg)
                dst = xfm[p].rearrange("p (c t) -> p c t", c=NCH)
                e.tensor_copy(out=dst[:, 4 * h:4 * h + 4, j * 128:(j + 1) * 128],
                              in_=psAB[bk][:].rearrange("p (c t) -> p c t", c=4)
                              ).then_inc(vch.h, 1)
            P.v(v_evac_tp)
            bank_free[bk] = vch.inc(1)
    state[f"l1tp_{si}"] = tch.n
    t_xfm_ready = vch.n

    plist = [("k", True), ("v", False)] + ([("q", True)] if do_q else [])
    for (nm, phi) in plist:
        wbf, t_w = ctx[f"w_{nm}"]
        bcol = ctx["bias_fm"]
        boff = {"k": 0, "v": NCH, "q": 2 * NCH}[nm]
        dstb = ctx[f"p_{nm}"][p]
        kb = ctx["kb"]
        mn = ctx["mn"]
        emb = ctx["emb"]
        for e_ch in range(NCH):
            bk = e_ch % 2
            t_bankfree = bank_free[bk]

            def t_proj(e, p=p, e_ch=e_ch, bk=bk, wbf=wbf, t_w=t_w,
                       t_bankfree=t_bankfree, t_x=t_xfm_ready):
                e.wait_ge(vch.h, max(t_x, t_w, t_bankfree))
                for c in range(NCH):
                    ins = e.matmul(
                        out=psAB[bk][:, 0:TILE],
                        lhsT=wbf[:, c * D + e_ch * 128:c * D + (e_ch + 1) * 128],
                        rhs=xfm[p][:, c * TILE:(c + 1) * TILE],
                        start=(c == 0), stop=(c == NCH - 1))
                ins.then_inc(tch.h, 1)
            P.t(t_proj)
            t_mm = tch.inc(1)

            if not phi:
                def s_ev(e, e_ch=e_ch, bk=bk, dstb=dstb, boff=boff, t_mm=t_mm,
                         t_b=ctx["t_bias"]):
                    e.wait_ge(tch.h, t_mm)
                    wge(e, vch.h, t_b)
                    e.activation(out=dstb[:, e_ch * TILE:(e_ch + 1) * TILE],
                                 in_=psAB[bk][:, 0:TILE], func=AF.Identity,
                                 bias=bcol[:, boff + e_ch:boff + e_ch + 1]
                                 ).then_inc(sch.h, 1)
                P.s(s_ev)
                t_sev = sch.inc(1)

                def v_done(e, t_sev=t_sev):
                    e.wait_ge(sch.h, t_sev)
                    e.engine_nop().then_inc(vch.h, 1)
                P.v(v_done)
                bank_free[bk] = vch.inc(1)
            else:
                t_kb_free = state.get("phi_last", 0)

                def s_ev(e, e_ch=e_ch, bk=bk, boff=boff, t_mm=t_mm,
                         t_b=ctx["t_bias"], t_kb_free=t_kb_free):
                    e.wait_ge(tch.h, t_mm)
                    wge(e, vch.h, max(t_b, t_kb_free))
                    e.activation(out=kb[:, 0:TILE], in_=psAB[bk][:, 0:TILE],
                                 func=AF.Identity,
                                 bias=bcol[:, boff + e_ch:boff + e_ch + 1]
                                 ).then_inc(sch.h, 1)
                P.s(s_ev)
                t_kbr = sch.inc(1)

                def v_mn(e, t_kbr=t_kbr):
                    e.wait_ge(sch.h, t_kbr)
                    e.tensor_scalar_min(out=mn[:, 0:TILE], in0=kb[:, 0:TILE],
                                        scalar1=0.0).then_inc(vch.h, 1)
                P.v(v_mn)
                t_mn = vch.inc(1)
                bank_free[bk] = t_mn

                def s_em(e, t_mn=t_mn):
                    e.wait_ge(vch.h, t_mn)
                    e.activation(out=emb[:, 0:TILE], in_=mn[:, 0:TILE],
                                 func=AF.Exp).then_inc(sch.h, 1)
                P.s(s_em)
                t_em = sch.inc(1)

                def v_a(e, t_em=t_em):
                    e.wait_ge(sch.h, t_em)
                    e.tensor_scalar_add(out=mn[:, 0:TILE], in0=kb[:, 0:TILE],
                                        scalar1=1.0).then_inc(vch.h, 1)
                P.v(v_a)
                t_a = vch.inc(1)

                def v_phi(e, e_ch=e_ch, dstb=dstb, t_a=t_a):
                    e.wait_ge(vch.h, t_a)
                    e.tensor_tensor(out=dstb[:, e_ch * TILE:(e_ch + 1) * TILE],
                                    in0=mn[:, 0:TILE], in1=emb[:, 0:TILE],
                                    op=ALU.max).then_inc(vch.h, 1)
                P.v(v_phi)
                state["phi_last"] = vch.inc(1)
    t_proj_all = vch.n
    t_proj_all_s = sch.n

    kt_tm = ctx["kt_tm"][p]
    va_tm = ctx["va_tm"][p]
    for cch in range(4):
        for pr in range(NCH):
            bk = 0
            t_bankfree = state.get("ptp_ev_last", 0)

            def t_tpp(e, p=p, cch=cch, pr=pr, bk=bk, t_bankfree=t_bankfree,
                      t_pa=t_proj_all, t_ps=t_proj_all_s):
                e.wait_ge(vch.h, max(t_pa, t_bankfree))
                e.wait_ge(sch.h, t_ps)
                pk = ctx["p_k"][p]
                pv = ctx["p_v"][p]
                psT = ctx["psT"]
                e.transpose(out=psT[bk][:, 0:128],
                            in_=pk[:, pr * 512 + cch * 128:pr * 512 + (cch + 1) * 128],
                            identity=ident_bf[:])
                e.transpose(out=psT[bk][:, 128:256],
                            in_=pv[:, pr * 512 + cch * 128:pr * 512 + (cch + 1) * 128],
                            identity=ident_bf[:]).then_inc(tch.h, 1)
            P.t(t_tpp)
            t_tpg = tch.inc(1)

            def v_evpp(e, p=p, cch=cch, pr=pr, bk=bk, t_tpg=t_tpg,
                       t_vo=ctx["t_vones"]):
                e.wait_ge(tch.h, t_tpg)
                wge(e, vch.h, t_vo)
                psT = ctx["psT"]
                e.tensor_copy(out=kt_tm[:, cch * 1024 + pr * 128:cch * 1024 + (pr + 1) * 128],
                              in_=psT[bk][:, 0:128])
                dst = va_tm[:, cch * 1040 + pr * 130:]
                e.tensor_copy(
                    out=bass.AP(va_tm.tensor, va_tm.offset + cch * 1040 + pr * 130,
                                [[va_tm.ap[0][0], 128], [65, 2], [1, 64]])
                    if False else
                    va_tm.rearrange("p (cc hh x) -> p cc hh x", cc=4, hh=16)[:, cch, 2 * pr:2 * pr + 2, 0:64],
                    in_=psT[bk][:, 128:256].rearrange("p (hh x) -> p hh x", hh=2)
                    ).then_inc(vch.h, 1)
            P.v(v_evpp)
            state["ptp_ev_last"] = vch.inc(1)
    state[f"l1ptp_{si}"] = tch.n
    return vch.n


def _l1_common(nc, es, P, T_own, with_q, with_wo, nbuf=2, abf_sz=60000,
               af_sz=10100):
    ctx = {}
    vch = Ctr(es, nc, "vch"); sch = Ctr(es, nc, "sch")
    tch = Ctr(es, nc, "tch"); gch = Ctr(es, nc, "gch")
    ldw = Ctr(es, nc, "ldwq")
    ldx = [Ctr(es, nc, f"ldx{p}") for p in range(2)]
    ldb = Ctr(es, nc, "ldb")
    st_c = Ctr(es, nc, "st")
    ctx.update(nc=nc, es=es, P=P, vch=vch, sch=sch, tch=tch, gch=gch,
               ldx=ldx, ldb=ldb, st=st_c, state={}, nbuf=nbuf)

    consts = build_consts(nc, es, P, gch)
    ctx["ident_f32"] = consts["ident_f32"]
    ctx["ident_bf"] = consts["ident_bf"]
    ctx["ones_row"] = consts["ones_row"]
    ctx["t_consts"] = consts["ready"]

    abf = Arena(nc, es, "abf", BF16, abf_sz)
    af32 = Arena(nc, es, "af32", F32, af_sz)
    ctx["abf"] = abf; ctx["af32"] = af32

    names = ["k", "v"] + (["q"] if with_q else []) + (["o"] if with_wo else [])
    wtens = {}
    for nm in names:
        wtens[nm] = abf.carve(NCH * D)
    wstage = af32.carve(NCH * D)
    af32.reset()
    xst = [af32.carve(4 * D) for _ in range(nbuf)]
    ctx["xst"] = xst
    ctx["xfm"] = [abf.carve(NCH * 512) for _ in range(nbuf)]
    ctx["p_k"] = [abf.carve(NCH * 512) for _ in range(nbuf)]
    ctx["p_v"] = [abf.carve(NCH * 512) for _ in range(nbuf)]
    if with_q:
        ctx["p_q"] = [abf.carve(NCH * 512) for _ in range(nbuf)]
    ctx["kt_tm"] = [abf.carve(4 * 1024) for _ in range(nbuf)]
    ctx["va_tm"] = [abf.carve(4 * 1040) for _ in range(nbuf)]
    ctx["kb"] = af32.carve(512)
    ctx["mn"] = af32.carve(512)
    ctx["emb"] = af32.carve(512)

    ctx["lnb"] = mk_ln_bufs(nc, es, 4)
    ctx["bias_fm"] = es.enter_context(nc.sbuf_tensor("bias_fm", [128, 3 * NCH], F32))
    ctx["bstg"] = es.enter_context(nc.sbuf_tensor("bstg", [NCH, 128], F32))
    ctx["S_sb"] = es.enter_context(nc.sbuf_tensor("S_sb", [128, NCH * 65], F32))
    ctx["S_bf"] = es.enter_context(nc.sbuf_tensor("S_bf", [128, NCH * 65], BF16))

    psA = es.enter_context(nc.psum_tensor("psA", [128, 512], F32))
    psB = es.enter_context(nc.psum_tensor("psB", [128, 512], F32))
    ctx["psAB"] = [psA, psB]
    psT_single = es.enter_context(nc.psum_tensor("psT0", [128, 256], BF16))
    psS_single = es.enter_context(nc.psum_tensor("psS0", [128, 65], F32))
    ctx["psT"] = [psT_single, psT_single]
    ctx["psS"] = [psS_single, psS_single]
    ctx["bank_free"] = {0: 0, 1: 0}

    wsrc = ctx["wsrc"] = {}
    return ctx, wtens, wstage, ldw


def build_l1state(T_own):
    nc = bass.Bass("TRN2", target_bir_lowering=False)
    s2 = nc.dram_tensor("s2", [T_own, D], F32, kind="ExternalInput").ap()
    wkT = nc.dram_tensor("wkT", [D, D], F32, kind="ExternalInput").ap()
    wvT = nc.dram_tensor("wvT", [D, D], F32, kind="ExternalInput").ap()
    bk = nc.dram_tensor("bk", [D], F32, kind="ExternalInput").ap()
    bv = nc.dram_tensor("bv", [D], F32, kind="ExternalInput").ap()
    stout = nc.dram_tensor("state_loc", [D, 65], F32, kind="ExternalOutput").ap()

    es = contextlib.ExitStack()
    P = Prog(nc)
    ctx, wtens, wstage, ldw = _l1_common(nc, es, P, T_own, with_q=False,
                                         with_wo=False)
    vch, sch, tch, gch = ctx["vch"], ctx["sch"], ctx["tch"], ctx["gch"]
    state = ctx["state"]
    st_c = ctx["st"]
    ldb = ctx["ldb"]
    ctx["s2src"] = s2

    tw = load_weights_bf(P, wstage, [(wtens["k"], wkT, D), (wtens["v"], wvT, D)],
                         ldw, vch)
    ctx["w_k"] = (wtens["k"], tw[0])
    ctx["w_v"] = (wtens["v"], tw[1])
    ctx["t_wdone"] = tw[-1]

    def sy_b(e):
        e.dma_start(out=ctx["bstg"][:], in_=bk.rearrange("(n p) -> n p", p=128)
                    ).then_inc(ldb.h, 16)
    P.sy(sy_b)
    t_b1 = ldb.inc(16)

    def t_bt(e):
        e.wait_ge(ldb.h, t_b1)
        wge(e, gch.h, ctx["t_consts"])
        e.transpose(out=ctx["psAB"][0][:, 0:NCH], in_=ctx["bstg"][:],
                    identity=ctx["ident_f32"][0:NCH, 0:NCH]).then_inc(tch.h, 1)
    P.t(t_bt)
    t_b2 = tch.inc(1)

    def v_bt(e):
        e.wait_ge(tch.h, t_b2)
        e.tensor_copy(out=ctx["bias_fm"][:, 0:NCH],
                      in_=ctx["psAB"][0][:, 0:NCH]).then_inc(vch.h, 1)
    P.v(v_bt)
    t_b3 = vch.inc(1)

    def sy_b2(e):
        e.wait_ge(vch.h, t_b3)
        e.dma_start(out=ctx["bstg"][:], in_=bv.rearrange("(n p) -> n p", p=128)
                    ).then_inc(ldb.h, 16)
    P.sy(sy_b2)
    t_b4 = ldb.inc(16)

    def t_bt2(e):
        e.wait_ge(ldb.h, t_b4)
        e.transpose(out=ctx["psAB"][0][:, 0:NCH], in_=ctx["bstg"][:],
                    identity=ctx["ident_f32"][0:NCH, 0:NCH]).then_inc(tch.h, 1)
    P.t(t_bt2)
    t_b5 = tch.inc(1)

    def v_bt2(e):
        e.wait_ge(tch.h, t_b5)
        e.tensor_copy(out=ctx["bias_fm"][:, NCH:2 * NCH],
                      in_=ctx["psAB"][0][:, 0:NCH]).then_inc(vch.h, 1)
    P.v(v_bt2)
    ctx["t_bias"] = vch.inc(1)
    ctx["bank_free"][0] = ctx["t_bias"]

    def v_init(e):
        for p in range(2):
            va = ctx["va_tm"][p]
            e.memset(va.rearrange("p (cc hh x) -> p cc hh x", cc=4, hh=16)[:, :, :, 64:65], 1.0)
        e.memset(ctx["S_sb"][:], 0.0).then_inc(vch.h, 1)
    P.v(v_init)
    ctx["t_vones"] = vch.inc(1)

    nt = T_own // 512
    psS = ctx["psS"]
    for si in range(nt):
        t_ready = l1_proj_tiles(ctx, si, si * 512, do_q=False)
        p = si % 2
        kt_tm = ctx["kt_tm"][p]
        va_tm = ctx["va_tm"][p]
        for pr in range(NCH):
            pp = pr % 2
            t_psfree = state.get(f"sevac_any_{(si * NCH + pr) - 1}", 0)

            def t_sc(e, pr=pr, pp=pp, kt_tm=kt_tm, va_tm=va_tm,
                     t_ready=t_ready, t_psfree=t_psfree):
                e.wait_ge(vch.h, max(t_ready, t_psfree))
                for hh in range(2):
                    for cch in range(4):
                        lhsT = kt_tm[:, cch * 1024 + pr * 128 + hh * 64:
                                     cch * 1024 + pr * 128 + (hh + 1) * 64]
                        rhs = va_tm[:, cch * 1040 + pr * 130 + hh * 65:
                                    cch * 1040 + pr * 130 + (hh + 1) * 65]
                        ins = e.matmul(out=psS[pp][hh * 64:(hh + 1) * 64, 0:65],
                                       lhsT=lhsT, rhs=rhs,
                                       start=(cch == 0), stop=(cch == 3))
                ins.then_inc(tch.h, 1)
            P.t(t_sc)
            t_scm = tch.inc(1)

            def v_sacc(e, si=si, pr=pr, pp=pp, t_scm=t_scm):
                e.wait_ge(tch.h, t_scm)
                if si == 0:
                    e.tensor_copy(out=ctx["S_sb"][:, pr * 65:(pr + 1) * 65],
                                  in_=psS[pp][:, 0:65]).then_inc(vch.h, 1)
                else:
                    e.tensor_tensor(out=ctx["S_sb"][:, pr * 65:(pr + 1) * 65],
                                    in0=ctx["S_sb"][:, pr * 65:(pr + 1) * 65],
                                    in1=psS[pp][:, 0:65],
                                    op=ALU.add).then_inc(vch.h, 1)
            P.v(v_sacc)
            state[f"sevac_any_{si * NCH + pr}"] = vch.inc(1)
    t_all = state[f"sevac_any_{nt * NCH - 1}"]

    t_out_target = st_c.inc(16)

    def sy_out(e, t_all=t_all, t_out_target=t_out_target):
        e.wait_ge(vch.h, t_all)
        e.dma_start(out=stout.rearrange("(n p) m -> p n m", p=128),
                    in_=ctx["S_sb"][:].rearrange("p (n m) -> p n m", n=NCH)
                    ).then_inc(st_c.h, 16)
        e.wait_ge(st_c.h, t_out_target)
    P.sy(sy_out)

    block = es.enter_context(nc.Block())
    P.emit(block)
    es.close()
    return nc


def build_l1attn(T_own):
    EPS = 1e-6
    nc = bass.Bass("TRN2", target_bir_lowering=False)
    s2 = nc.dram_tensor("s2", [T_own, D], F32, kind="ExternalInput").ap()
    wqT = nc.dram_tensor("wqT", [D, D], F32, kind="ExternalInput").ap()
    wkT = nc.dram_tensor("wkT", [D, D], F32, kind="ExternalInput").ap()
    wvT = nc.dram_tensor("wvT", [D, D], F32, kind="ExternalInput").ap()
    woT = nc.dram_tensor("woT", [D, D], F32, kind="ExternalInput").ap()
    bq = nc.dram_tensor("bq", [D], F32, kind="ExternalInput").ap()
    bk = nc.dram_tensor("bk", [D], F32, kind="ExternalInput").ap()
    bv = nc.dram_tensor("bv", [D], F32, kind="ExternalInput").ap()
    bo = nc.dram_tensor("bo", [D], F32, kind="ExternalInput").ap()
    stv = nc.dram_tensor("state_v", [64, 1024], F32, kind="ExternalInput").ap()
    stk = nc.dram_tensor("state_k", [64, 16], F32, kind="ExternalInput").ap()
    s3 = nc.dram_tensor("s3", [T_own, D], F32, kind="ExternalOutput").ap()

    es = contextlib.ExitStack()
    P = Prog(nc)
    ctx, wtens, wstage, ldw = _l1_common(nc, es, P, T_own, with_q=True,
                                         with_wo=True, nbuf=1, abf_sz=68160,
                                         af_sz=16160)
    vch, sch, tch, gch = ctx["vch"], ctx["sch"], ctx["tch"], ctx["gch"]
    state = ctx["state"]
    st_c = ctx["st"]
    ldb = ctx["ldb"]
    ldxr = Ctr(es, nc, "ldxr")
    ldsi = Ctr(es, nc, "ldsi")
    ldsh = Ctr(es, nc, "ldsh")
    ctx["s2src"] = s2
    abf, af32 = ctx["abf"], ctx["af32"]
    psAB = ctx["psAB"]
    psT = ctx["psT"]
    bank_free = ctx["bank_free"]
    lnb = ctx["lnb"]

    tw = load_weights_bf(P, wstage, [(wtens["k"], wkT, D), (wtens["v"], wvT, D),
                                     (wtens["q"], wqT, D), (wtens["o"], woT, D)],
                         ldw, vch)
    ctx["w_k"] = (wtens["k"], tw[0])
    ctx["w_v"] = (wtens["v"], tw[1])
    ctx["w_q"] = (wtens["q"], tw[2])
    wo_bf, t_wo = wtens["o"], tw[3]
    ctx["t_wdone"] = tw[-1]

    attn_buf = abf.carve(4 * 1024)
    afm = abf.carve(NCH * 512)
    q_od = afm
    pt_sb = [abf.carve(512) for _ in range(2)]
    mask4 = abf.carve(512)
    out2 = af32.carve(4 * D)
    xres = af32.carve(4 * D)
    och = af32.carve(288)
    bo_bc = abf.carve(D)
    k_od = af32.carve(2048).bitcast(BF16)
    rden4 = och[:, 264:268]
    borow = xres[0:1, 0:D]
    S_v = es.enter_context(nc.sbuf_tensor("S_v", [64, 1024], F32))
    S_k = es.enter_context(nc.sbuf_tensor("S_k", [64, 16], F32))
    S_vb = es.enter_context(nc.sbuf_tensor("S_vb", [64, 1024], BF16))
    S_kb = es.enter_context(nc.sbuf_tensor("S_kb", [64, 16], BF16))
    psO4 = [es.enter_context(nc.psum_tensor(f"psO4_{i}", [128, 260], F32))
            for i in range(2)]
    psSv = es.enter_context(nc.psum_tensor("psSv", [64, 512], F32))
    psSk = es.enter_context(nc.psum_tensor("psSk", [64, 16], F32))
    ones_col_bf = es.enter_context(nc.sbuf_tensor("ones_col_bf", [128, 1], BF16))

    bsrc = {"k": bk, "v": bv, "q": bq}
    boffm = {"k": 0, "v": NCH, "q": 2 * NCH}
    t_prev = 0
    for nm in ("k", "v", "q"):
        t_l = ldb.inc(16)

        def sy_b(e, nm=nm, t_prev=t_prev):
            wge(e, vch.h, t_prev)
            e.dma_start(out=ctx["bstg"][:],
                        in_=bsrc[nm].rearrange("(n p) -> n p", p=128)
                        ).then_inc(ldb.h, 16)
        P.sy(sy_b)

        def t_bt(e, t_l=t_l):
            e.wait_ge(ldb.h, t_l)
            wge(e, gch.h, ctx["t_consts"])
            e.transpose(out=psAB[0][:, 0:NCH], in_=ctx["bstg"][:],
                        identity=ctx["ident_f32"][0:NCH, 0:NCH]).then_inc(tch.h, 1)
        P.t(t_bt)
        t_t = tch.inc(1)

        def v_bt(e, nm=nm, t_t=t_t):
            e.wait_ge(tch.h, t_t)
            e.tensor_copy(out=ctx["bias_fm"][:, boffm[nm]:boffm[nm] + NCH],
                          in_=psAB[0][:, 0:NCH]).then_inc(vch.h, 1)
        P.v(v_bt)
        t_prev = vch.inc(1)
    ctx["t_bias"] = t_prev
    bank_free[0] = t_prev

    t_bo = ldb.inc(16)

    def sy_bo(e, t_bo_prev=t_prev):
        wge(e, vch.h, t_bo_prev)
        e.dma_start(out=borow[:], in_=bo.rearrange("(a d) -> a d", a=1)
                    ).then_inc(ldb.h, 16)
    P.sy(sy_bo)

    def t_bobc(e, t_bo=t_bo, t_bf=bank_free[0]):
        e.wait_ge(ldb.h, t_bo)
        wge(e, gch.h, ctx["t_consts"])
        wge(e, vch.h, t_bf)
        e.matmul(out=psAB[0][:, 0:512], lhsT=ctx["ones_row"][:],
                 rhs=borow[0:1, 0:512], start=True, stop=True)
        e.matmul(out=psAB[1][:, 0:512], lhsT=ctx["ones_row"][:],
                 rhs=borow[0:1, 512:1024], start=True, stop=True).then_inc(tch.h, 1)
    P.t(t_bobc)
    t_bob = tch.inc(1)

    def v_bobc(e, t_bob=t_bob):
        e.wait_ge(tch.h, t_bob)
        e.tensor_copy(out=bo_bc[:, 0:512], in_=psAB[0][:, 0:512])
        e.tensor_copy(out=bo_bc[:, 512:1024], in_=psAB[1][:, 0:512]).then_inc(vch.h, 1)
    P.v(v_bobc)
    t_bobc_v = vch.inc(1)
    bank_free[0] = t_bobc_v
    bank_free[1] = t_bobc_v

    def sy_si(e):
        e.dma_start(out=S_v[:], in_=stv).then_inc(ldsi.h, 16)
        e.dma_start(out=S_k[:], in_=stk).then_inc(ldsi.h, 16)
    P.sy(sy_si)
    t_si = ldsi.inc(32)

    def v_init(e):
        e.wait_ge(ldsi.h, t_si)
        for p in range(ctx["nbuf"]):
            va = ctx["va_tm"][p]
            e.memset(va.rearrange("p (cc hh x) -> p cc hh x", cc=4, hh=16)[:, :, :, 64:65], 1.0)
        e.memset(ones_col_bf[:], 1.0)
        e.tensor_copy(out=S_vb[:], in_=S_v[:])
        e.tensor_copy(out=S_kb[:], in_=S_k[:])
        e.memset(mask4, 1.0).then_inc(vch.h, 1)
    P.v(v_init)
    ctx["t_vones"] = vch.inc(1)
    t_vinit = ctx["t_vones"]

    def g_mask(e, t_vinit=t_vinit):
        e.wait_ge(vch.h, t_vinit)
        e.affine_select(out=mask4, in_=mask4, pattern=[[0, 4], [1, 128]],
                        compare_op=ALU.is_ge, fill=0.0,
                        base=0, channel_multiplier=-1).then_inc(gch.h, 1)
    P.g(g_mask)
    t_mask = gch.inc(1)

    nt = T_own // 512
    s3_stored = 0
    t_supd_prev = t_vinit
    for si in range(nt):
        t_ready = l1_proj_tiles(ctx, si, si * 512, do_q=True)
        p = 0
        kt_tm = ctx["kt_tm"][p]
        va_tm = ctx["va_tm"][p]
        p_q = ctx["p_q"][p]
        p_k = ctx["p_k"][p]

        t_qod_used = state.get(f"afm_used_{si - 1}", 0)
        t_kod_used = state.get(f"kod_used_{si - 1}", 0)

        def sy_shift(e, t_ready=t_ready, t_qod_used=t_qod_used,
                     t_kod_used=t_kod_used):
            e.wait_ge(vch.h, t_ready)
            wge(e, tch.h, max(t_qod_used, t_kod_used))
            e.dma_start(out=q_od[0:64, :], in_=p_q[64:128, :]).then_inc(ldsh.h, 16)
            e.dma_start(out=k_od[0:64, :], in_=p_k[64:128, :]).then_inc(ldsh.h, 16)
        P.sy(sy_shift)
        t_shift = ldsh.inc(32)

        def v_shift(e, t_shift=t_shift):
            e.wait_ge(ldsh.h, t_shift)
            e.engine_nop().then_inc(vch.h, 1)
        P.v(v_shift)
        t_shift_v = vch.inc(1)

        def ksl(h, cch):
            src = p_k if h % 2 == 0 else k_od
            return src[0:64, (h // 2) * 512 + cch * 128:(h // 2) * 512 + (cch + 1) * 128]

        def qsl(h, cch):
            src = p_q if h % 2 == 0 else q_od
            return src[0:64, (h // 2) * 512 + cch * 128:(h // 2) * 512 + (cch + 1) * 128]

        for cch in range(4):
            for g4 in range(4):
                bk2 = g4 % 2
                t_bankfree = bank_free[bk2]
                t_pt_free = state.get(f"pvmm_{si}_{cch}_{g4 - 2}", 0) if g4 >= 2 \
                    else state.get(f"pvmm_{si}_{cch - 1}_{g4 + 2}", 0)

                def t_scor(e, cch=cch, g4=g4, bk2=bk2, t_bankfree=t_bankfree,
                           t_sh=t_shift_v):
                    e.wait_ge(vch.h, max(t_sh, t_bankfree))
                    for hh4 in range(4):
                        h = g4 * 4 + hh4
                        ins = e.matmul(out=psAB[bk2][:, hh4 * 128:(hh4 + 1) * 128],
                                       lhsT=ksl(h, cch), rhs=qsl(h, cch),
                                       start=True, stop=True)
                    ins.then_inc(tch.h, 1)
                P.t(t_scor)
                t_sg = tch.inc(1)

                def v_mask(e, g4=g4, bk2=bk2, t_sg=t_sg, t_mask=t_mask,
                           t_pt_free=t_pt_free):
                    e.wait_ge(tch.h, t_sg)
                    wge(e, gch.h, t_mask)
                    wge(e, tch.h, t_pt_free)
                    e.scalar_tensor_tensor(out=pt_sb[g4 % 2], in0=psAB[bk2][:, 0:512],
                                           scalar=1.0, in1=mask4,
                                           op0=ALU.mult, op1=ALU.mult
                                           ).then_inc(vch.h, 1)
                P.v(v_mask)
                bank_free[bk2] = vch.inc(1)
                t_pt = bank_free[bk2]

                om = g4 % 2
                t_o4free = state.get(f"oevac_{si}_{cch}_{g4 - 2}", 0) if g4 >= 2 \
                    else state.get(f"oevac_{si}_{cch - 1}_{g4 + 2}", 0)

                def t_pv(e, cch=cch, g4=g4, om=om, t_pt=t_pt, t_o4free=t_o4free,
                         t_supd=t_supd_prev):
                    e.wait_ge(vch.h, max(t_pt, t_o4free, t_supd))
                    for hh4 in range(4):
                        h = g4 * 4 + hh4
                        pr = h // 2
                        e.matmul(out=psO4[om][:, hh4 * 65:(hh4 + 1) * 65],
                                 lhsT=pt_sb[g4 % 2][:, hh4 * 128:(hh4 + 1) * 128],
                                 rhs=va_tm[:, cch * 1040 + pr * 130 + (h % 2) * 65:
                                           cch * 1040 + pr * 130 + (h % 2 + 1) * 65],
                                 start=True, stop=False)
                        e.matmul(out=psO4[om][:, hh4 * 65:hh4 * 65 + 64],
                                 lhsT=qsl(h, cch),
                                 rhs=S_vb[0:64, h * 64:(h + 1) * 64],
                                 start=False, stop=False)
                        ins = e.matmul(out=psO4[om][:, hh4 * 65 + 64:hh4 * 65 + 65],
                                       lhsT=qsl(h, cch),
                                       rhs=S_kb[0:64, h:h + 1],
                                       start=False, stop=True)
                    ins.then_inc(tch.h, 1)
                P.t(t_pv)
                state[f"pvmm_{si}_{cch}_{g4}"] = tch.inc(1)
                t_pvm = state[f"pvmm_{si}_{cch}_{g4}"]

                t_och_free = state.get("t_och", 0)

                def v_oe(e, g4=g4, om=om, t_pvm=t_pvm, t_och_free=t_och_free):
                    e.wait_ge(tch.h, t_pvm)
                    wge(e, vch.h, t_och_free)
                    e.tensor_copy(out=och[:, 0:260], in_=psO4[om][:, 0:260]
                                  ).then_inc(vch.h, 1)
                P.v(v_oe)
                state[f"oevac_{si}_{cch}_{g4}"] = vch.inc(1)
                t_oe = state[f"oevac_{si}_{cch}_{g4}"]

                def v_eps(e, t_oe=t_oe):
                    e.wait_ge(vch.h, t_oe)
                    e.tensor_scalar_add(out=och[:, 64:260:65], in0=och[:, 64:260:65],
                                        scalar1=EPS).then_inc(vch.h, 1)
                P.v(v_eps)
                t_ep = vch.inc(1)

                def v_rec(e, t_ep=t_ep):
                    e.wait_ge(vch.h, t_ep)
                    e.reciprocal(out=rden4[:, 0:4], in_=och[:, 64:260:65]
                                 ).then_inc(vch.h, 1)
                P.v(v_rec)
                t_rc = vch.inc(1)

                t_abuf_used = state.get("abuf_used", 0)

                def v_div(e, cch=cch, g4=g4, t_rc=t_rc, t_abuf_used=t_abuf_used):
                    e.wait_ge(vch.h, t_rc)
                    wge(e, tch.h, t_abuf_used)
                    for hh4 in range(4):
                        h = g4 * 4 + hh4
                        ins = e.tensor_scalar_mul(
                            out=attn_buf[:, cch * 1024 + h * 64:cch * 1024 + (h + 1) * 64],
                            in0=och[:, hh4 * 65:hh4 * 65 + 64],
                            scalar1=rden4[:, hh4:hh4 + 1])
                    ins.then_inc(vch.h, 1)
                P.v(v_div)
                state["t_och"] = vch.inc(1)
            t_pv_all = state[f"pvmm_{si}_{cch}_{3}"]

            for half in range(2):
                t_svfree = state.get(f"svev_{(si * 4 + cch) * 2 + half - 1}", 0)

                def t_sv(e, cch=cch, half=half, t_ready=t_ready,
                         t_svfree=t_svfree, t_pv_all=t_pv_all):
                    e.wait_ge(vch.h, max(t_ready, t_svfree))
                    wge(e, tch.h, t_pv_all)
                    for hh in range(8):
                        h = half * 8 + hh
                        pr = h // 2
                        ins = e.matmul(
                            out=psSv[:, hh * 64:(hh + 1) * 64],
                            lhsT=kt_tm[:, cch * 1024 + pr * 128 + (h % 2) * 64:
                                       cch * 1024 + pr * 128 + (h % 2 + 1) * 64],
                            rhs=va_tm[:, cch * 1040 + pr * 130 + (h % 2) * 65:
                                      cch * 1040 + pr * 130 + (h % 2) * 65 + 64],
                            start=True, stop=True)
                    ins.then_inc(tch.h, 1)
                P.t(t_sv)
                t_svm = tch.inc(1)

                def v_svacc(e, half=half, t_svm=t_svm):
                    e.wait_ge(tch.h, t_svm)
                    e.tensor_tensor(out=S_v[:, half * 512:(half + 1) * 512],
                                    in0=S_v[:, half * 512:(half + 1) * 512],
                                    in1=psSv[:, 0:512], op=ALU.add
                                    ).then_inc(vch.h, 1)
                P.v(v_svacc)
                state[f"svev_{(si * 4 + cch) * 2 + half}"] = vch.inc(1)
            t_sv_done = state[f"svev_{(si * 4 + cch) * 2 + 1}"]

            t_skfree = state.get(f"skev_{si * 4 + cch - 1}", 0)

            def t_sk(e, cch=cch, t_ready=t_ready, t_skfree=t_skfree,
                     t_pv_all=t_pv_all, t_vo=t_vinit):
                e.wait_ge(vch.h, max(t_ready, t_skfree, t_vo))
                wge(e, tch.h, t_pv_all)
                for h in range(16):
                    pr = h // 2
                    ins = e.matmul(
                        out=psSk[:, h:h + 1],
                        lhsT=kt_tm[:, cch * 1024 + pr * 128 + (h % 2) * 64:
                                   cch * 1024 + pr * 128 + (h % 2 + 1) * 64],
                        rhs=ones_col_bf[:], start=True, stop=True)
                ins.then_inc(tch.h, 1)
            P.t(t_sk)
            t_skm = tch.inc(1)

            def v_skacc(e, t_skm=t_skm):
                e.wait_ge(tch.h, t_skm)
                e.tensor_tensor(out=S_k[:], in0=S_k[:], in1=psSk[:, 0:16],
                                op=ALU.add).then_inc(vch.h, 1)
            P.v(v_skacc)
            state[f"skev_{si * 4 + cch}"] = vch.inc(1)
            t_sk_done = state[f"skev_{si * 4 + cch}"]

            def v_sbf(e, t_sv_done=t_sv_done, t_sk_done=t_sk_done):
                e.wait_ge(vch.h, max(t_sv_done, t_sk_done))
                e.tensor_copy(out=S_vb[:], in_=S_v[:])
                e.tensor_copy(out=S_kb[:], in_=S_k[:]).then_inc(vch.h, 1)
            P.v(v_sbf)
            t_supd_prev = vch.inc(1)

        t_attn_all = state["t_och"]
        for cch in range(4):
            for ech in range(NCH):
                bk2 = 0
                t_tfree = state.get(f"afm_ev_{(cch * NCH + ech) - 1}", 0)

                def t_at(e, cch=cch, ech=ech, bk2=bk2, t_attn_all=t_attn_all,
                         t_tfree=t_tfree, si=si):
                    e.wait_ge(vch.h, max(t_attn_all, t_tfree))
                    e.transpose(out=psT[bk2][:, 0:128],
                                in_=attn_buf[:, cch * 1024 + ech * 128:
                                             cch * 1024 + (ech + 1) * 128],
                                identity=ctx["ident_bf"][:]).then_inc(tch.h, 1)
                P.t(t_at)
                t_tg = tch.inc(1)

                def v_ae(e, cch=cch, ech=ech, bk2=bk2, t_tg=t_tg):
                    e.wait_ge(tch.h, t_tg)
                    e.tensor_copy(out=afm[:, ech * 512 + cch * 128:
                                          ech * 512 + (cch + 1) * 128],
                                  in_=psT[bk2][:, 0:128]).then_inc(vch.h, 1)
                P.v(v_ae)
                state[f"afm_ev_{cch * NCH + ech}"] = vch.inc(1)
        t_afm = state[f"afm_ev_{4 * NCH - 1}"]
        state["abuf_used"] = tch.n

        for tb in range(4):
            t_bf0 = bank_free[0]
            t_bf1 = bank_free[1]

            def t_om(e, tb=tb, t_afm=t_afm, t_wo=t_wo, t_bf0=t_bf0, t_bf1=t_bf1):
                e.wait_ge(vch.h, max(t_afm, t_wo, t_bf0, t_bf1))
                for ech in range(NCH):
                    lh = afm[:, ech * 512 + tb * 128:ech * 512 + (tb + 1) * 128]
                    e.matmul(out=psAB[0][:, 0:512], lhsT=lh,
                             rhs=wo_bf[:, ech * D:ech * D + 512],
                             start=(ech == 0), stop=(ech == NCH - 1))
                    ins = e.matmul(out=psAB[1][:, 0:512], lhsT=lh,
                                   rhs=wo_bf[:, ech * D + 512:ech * D + 1024],
                                   start=(ech == 0), stop=(ech == NCH - 1))
                ins.then_inc(tch.h, 1)
            P.t(t_om)
            t_omm = tch.inc(1)
            if tb == 3:
                state[f"afm_used_{si}"] = t_omm
                state[f"kod_used_{si}"] = state[f"pvmm_{si}_{3}_{3}"]

            def v_oev(e, tb=tb, t_omm=t_omm, t_bo=t_bobc_v, t_s3=s3_stored):
                e.wait_ge(tch.h, t_omm)
                wge(e, vch.h, t_bo)
                wge(e, st_c.h, t_s3)
                e.tensor_tensor(out=out2[:, tb * D:tb * D + 512],
                                in0=psAB[0][:, 0:512], in1=bo_bc[:, 0:512],
                                op=ALU.add)
                e.tensor_tensor(out=out2[:, tb * D + 512:(tb + 1) * D],
                                in0=psAB[1][:, 0:512], in1=bo_bc[:, 512:1024],
                                op=ALU.add).then_inc(vch.h, 1)
            P.v(v_oev)
            bank_free[0] = vch.inc(1)
            bank_free[1] = bank_free[0]
        t_out2 = bank_free[0]

        def v_pre(e, t_out2=t_out2):
            e.wait_ge(vch.h, t_out2)
        P.v(v_pre)
        t_ap2 = emit_ln(P, lnb, vch, sch, nblk=4,
                        src=out2, srcol=lambda j: j * D,
                        dst=out2, dstcol=lambda j: j * D)

        def sy_xr(e, si=si, t_s3=s3_stored):
            wge(e, st_c.h, t_s3)
            if si == 0:
                wge(e, vch.h, t_bobc_v)
            e.dma_start(out=xres.rearrange("p (j c) -> p j c", j=4),
                        in_=s2[si * 512:(si + 1) * 512, :].rearrange(
                            "(j p) c -> p j c", p=128)).then_inc(ldxr.h, 16)
        P.sy(sy_xr)
        t_xr = ldxr.inc(16)

        def v_res(e, t_ap2=t_ap2, t_xr=t_xr, t_s3=s3_stored):
            e.wait_ge(sch.h, t_ap2)
            e.wait_ge(ldxr.h, t_xr)
            wge(e, st_c.h, t_s3)
            e.tensor_tensor(out=out2[:, 0:4 * D], in0=out2[:, 0:4 * D],
                            in1=xres[:, 0:4 * D], op=ALU.add).then_inc(vch.h, 1)
        P.v(v_res)
        t_rv = vch.inc(1)

        def sy_s3(e, si=si, t_rv=t_rv):
            e.wait_ge(vch.h, t_rv)
            e.dma_start(out=s3[si * 512:(si + 1) * 512, :].rearrange(
                "(j p) c -> p j c", p=128),
                in_=out2.rearrange("p (j c) -> p j c", j=4)).then_inc(st_c.h, 16)
        P.sy(sy_s3)
        s3_stored = st_c.inc(16)

    t_fin = s3_stored

    def sy_fin(e, t=t_fin):
        e.wait_ge(st_c.h, t)
    P.sy(sy_fin)

    block = es.enter_context(nc.Block())
    P.emit(block)
    es.close()
    return nc




def build_l1ffn(T_own):
    nc = bass.Bass("TRN2", target_bir_lowering=False)
    s3e = nc.dram_tensor("s3e", [T_own + 1, D], F32, kind="ExternalInput").ap()
    wrT = nc.dram_tensor("wrT", [D, D], F32, kind="ExternalInput").ap()
    wkT = nc.dram_tensor("wkT", [D, D], F32, kind="ExternalInput").ap()
    wvT = nc.dram_tensor("wvT", [D, D], F32, kind="ExternalInput").ap()
    mu_r = nc.dram_tensor("mu_r", [D], F32, kind="ExternalInput").ap()
    mu_k = nc.dram_tensor("mu_k", [D], F32, kind="ExternalInput").ap()
    s4 = nc.dram_tensor("s4", [T_own, D], F32, kind="ExternalOutput").ap()

    es = contextlib.ExitStack()
    P = Prog(nc)
    vch = Ctr(es, nc, "vch"); sch = Ctr(es, nc, "sch")
    tch = Ctr(es, nc, "tch"); gch = Ctr(es, nc, "gch")
    ldw = Ctr(es, nc, "ldw")
    ldx = [Ctr(es, nc, f"ldx{p}") for p in range(2)]
    ldb = Ctr(es, nc, "ldb")
    ldxr = Ctr(es, nc, "ldxr")
    st_c = Ctr(es, nc, "st")

    consts = build_consts(nc, es, P, gch)
    ident_f32 = consts["ident_f32"]
    t_consts = consts["ready"]

    TP1 = T_own + 1
    abf = Arena(nc, es, "abf", BF16, 3 * NCH * D + NCH * TP1 + 3 * NCH * 512 + 4 * 1024 + 96)

    wr_bf = abf.carve(NCH * D)
    wk_bf = abf.carve(NCH * D)
    wv_bf = abf.carve(NCH * D)
    ln4 = abf.carve(NCH * TP1)
    xr_fm = abf.carve(NCH * 512)
    xk_fm = abf.carve(NCH * 512)
    k2_fm = abf.carve(NCH * 512)
    r_tm = abf.carve(4 * 1024)

    af32b = Arena(nc, es, "af32b", F32, 13100)
    wstage = af32b.carve(NCH * D)
    af32b.reset()
    xst1 = af32b.carve(4 * D)
    xst = [xst1, xst1]
    out2 = af32b.carve(4 * D)
    xres = af32b.carve(4 * D)
    tmpE = af32b.carve(512)

    lnb = mk_ln_bufs(nc, es, 4)
    mub = es.enter_context(nc.sbuf_tensor("mub", [128, 2 * NCH], F32))
    bstg = es.enter_context(nc.sbuf_tensor("bstg", [NCH, 128], F32))
    dbuf = es.enter_context(nc.sbuf_tensor("dbuf", [128, 512], BF16))

    psA = es.enter_context(nc.psum_tensor("psA", [128, 512], F32))
    psB = es.enter_context(nc.psum_tensor("psB", [128, 512], F32))
    psC = es.enter_context(nc.psum_tensor("psC", [128, 512], F32))
    psD = es.enter_context(nc.psum_tensor("psD", [128, 512], F32))
    psAB = [psA, psB]
    psCD = [psC, psD]

    tw = load_weights_bf(P, wstage, [(wr_bf, wrT, D), (wk_bf, wkT, D),
                                     (wv_bf, wvT, D)], ldw, vch)
    t_wr, t_wk, t_wv = tw
    t_wdone = tw[-1]

    state = {}
    bank_free = {0: 0, 1: 0}
    t_prev = 0
    for (i, mu) in enumerate((mu_r, mu_k)):
        t_l = ldb.inc(16)

        def sy_mu(e, mu=mu, t_prev=t_prev):
            wge(e, vch.h, t_prev)
            e.dma_start(out=bstg[:], in_=mu.rearrange("(n p) -> n p", p=128)
                        ).then_inc(ldb.h, 16)
        P.sy(sy_mu)

        def t_mu(e, t_l=t_l):
            e.wait_ge(ldb.h, t_l)
            wge(e, gch.h, t_consts)
            e.transpose(out=psA[:, 0:NCH], in_=bstg[:],
                        identity=ident_f32[0:NCH, 0:NCH]).then_inc(tch.h, 1)
        P.t(t_mu)
        t_t = tch.inc(1)

        def v_mu(e, i=i, t_t=t_t):
            e.wait_ge(tch.h, t_t)
            e.tensor_copy(out=mub[:, i * NCH:(i + 1) * NCH],
                          in_=psA[:, 0:NCH]).then_inc(vch.h, 1)
        P.v(v_mu)
        t_prev = vch.inc(1)
    t_mu_done = t_prev
    bank_free[0] = t_mu_done

    n_wt = (TP1 + 511) // 512
    apply_done = {}
    for wi in range(n_wt):
        p = 0
        r0 = wi * 512
        rows = min(512, TP1 - r0)
        nfull = rows // 128
        rem = rows - nfull * 128
        blocks = [(j * D, j * D, 128) for j in range(nfull)]
        if rem:
            blocks.append((nfull * D, nfull * D, rem))
        t_apfree = apply_done.get(wi - 1, 0)
        t_tp_prev = state.get(f"etp_{wi - 1}", 0)

        def sy_ld(e, p=p, r0=r0, rows=rows, nfull=nfull, rem=rem,
                  t_apfree=t_apfree, t_tp_prev=t_tp_prev, wi=wi):
            if wi < 2:
                wge(e, vch.h, t_wdone)
            wge(e, sch.h, t_apfree)
            wge(e, tch.h, t_tp_prev)
            if nfull:
                e.dma_start(out=xst[p].rearrange("p (j c) -> p j c", j=4)[:, 0:nfull, :],
                            in_=s3e[r0:r0 + nfull * 128, :].rearrange(
                                "(j p) c -> p j c", p=128)).then_inc(ldx[p].h, 16)
            if rem:
                e.dma_start(out=xst[p][0:rem, nfull * D:(nfull + 1) * D],
                            in_=s3e[r0 + nfull * 128:r0 + rows, :]).then_inc(ldx[p].h, 16)
        P.sy(sy_ld)
        n_dma = (1 if nfull else 0) + (1 if rem else 0)
        t_ldx = ldx[p].inc(16 * n_dma)

        def v_pre(e, p=p, t_ldx=t_ldx):
            e.wait_ge(ldx[p].h, t_ldx)
        P.v(v_pre)
        t_apply = emit_ln_blocks(P, lnb, vch, sch, src=xst[p], dst=xst[p],
                                 blocks=blocks, war_s=(tch.h, t_tp_prev))
        apply_done[wi] = t_apply

        nblk_t = nfull + (1 if rem else 0)
        for j in range(nblk_t):
            rr = 128 if j < nfull else rem
            for h in range(2):
                bk = (j * 2 + h) % 2
                t_bankfree = bank_free[bk]

                def t_tp(e, p=p, j=j, h=h, rr=rr, bk=bk, t_bankfree=t_bankfree,
                         t_ap=t_apply):
                    e.wait_ge(sch.h, t_ap)
                    wge(e, gch.h, t_consts)
                    wge(e, vch.h, t_bankfree)
                    for cc in range(4):
                        c = 4 * h + cc
                        ins = e.transpose(
                            out=psAB[bk][:, cc * 128:cc * 128 + rr],
                            in_=xst[p][0:rr, j * D + c * 128:j * D + (c + 1) * 128],
                            identity=ident_f32[0:rr, 0:rr])
                    ins.then_inc(tch.h, 1)
                P.t(t_tp)
                t_tpg = tch.inc(1)

                def v_ev(e, p=p, j=j, h=h, rr=rr, bk=bk, r0=r0, t_tpg=t_tpg):
                    e.wait_ge(tch.h, t_tpg)
                    dst = ln4.rearrange("p (c t) -> p c t", c=NCH)
                    e.tensor_copy(out=dst[:, 4 * h:4 * h + 4, r0 + j * 128:r0 + j * 128 + rr],
                                  in_=psAB[bk][:].rearrange("p (c t) -> p c t", c=4)[:, :, 0:rr]
                                  ).then_inc(vch.h, 1)
                P.v(v_ev)
                bank_free[bk] = vch.inc(1)
        state[f"etp_{wi}"] = tch.n
    t_ln4 = vch.n

    s4_stored = 0
    for si in range(T_own // 512):
        t0 = si * 512
        t_xr_prev = state.get(f"emix_used_{si - 1}", 0)
        for c in range(NCH):
            t_dprev = state.get(f"emix_{si}_{c - 1}", 0)

            def v_d(e, c=c, t0=t0, t_ln4=t_ln4, t_xr_prev=t_xr_prev,
                    t_dprev=t_dprev):
                e.wait_ge(vch.h, max(t_ln4, t_dprev))
                wge(e, tch.h, t_xr_prev)
                e.tensor_tensor(out=dbuf[:],
                                in0=ln4[:, c * TP1 + t0 + 1:c * TP1 + t0 + 513],
                                in1=ln4[:, c * TP1 + t0:c * TP1 + t0 + 512],
                                op=ALU.subtract).then_inc(vch.h, 1)
            P.v(v_d)
            t_d = vch.inc(1)

            def v_xr(e, c=c, t0=t0, t_d=t_d, t_mu=t_mu_done):
                e.wait_ge(vch.h, max(t_d, t_mu))
                e.scalar_tensor_tensor(out=xr_fm[:, c * 512:(c + 1) * 512],
                                       in0=dbuf[:], scalar=mub[:, c:c + 1],
                                       in1=ln4[:, c * TP1 + t0:c * TP1 + t0 + 512],
                                       op0=ALU.mult, op1=ALU.add)
                e.scalar_tensor_tensor(out=xk_fm[:, c * 512:(c + 1) * 512],
                                       in0=dbuf[:], scalar=mub[:, NCH + c:NCH + c + 1],
                                       in1=ln4[:, c * TP1 + t0:c * TP1 + t0 + 512],
                                       op0=ALU.mult, op1=ALU.add).then_inc(vch.h, 1)
            P.v(v_xr)
            state[f"emix_{si}_{c}"] = vch.inc(1)
        t_mix = state[f"emix_{si}_{NCH - 1}"]

        for tb in range(4):
            bk = tb % 2
            t_bankfree = bank_free[bk]
            t_rtm_used = state.get(f"ekv_ev_{si - 1}_{tb}", 0)

            def t_wr_mm(e, tb=tb, bk=bk, t_mix=t_mix, t_w=t_wr,
                        t_bankfree=t_bankfree):
                e.wait_ge(vch.h, max(t_mix, t_w, t_bankfree))
                for c in range(NCH):
                    e.matmul(out=psAB[bk][:, 0:512],
                             lhsT=xr_fm[:, c * 512 + tb * 128:c * 512 + (tb + 1) * 128],
                             rhs=wr_bf[:, c * D:c * D + 512],
                             start=(c == 0), stop=(c == NCH - 1))
                    ins = e.matmul(out=psCD[bk][:, 0:512],
                                   lhsT=xr_fm[:, c * 512 + tb * 128:c * 512 + (tb + 1) * 128],
                                   rhs=wr_bf[:, c * D + 512:c * D + 1024],
                                   start=(c == 0), stop=(c == NCH - 1))
                ins.then_inc(tch.h, 1)
            P.t(t_wr_mm)
            t_mm = tch.inc(1)

            def s_sig(e, tb=tb, bk=bk, t_mm=t_mm, t_rtm_used=t_rtm_used):
                e.wait_ge(tch.h, t_mm)
                wge(e, vch.h, t_rtm_used)
                e.activation(out=r_tm[:, tb * D:tb * D + 512], in_=psAB[bk][:, 0:512],
                             func=AF.Sigmoid)
                e.activation(out=r_tm[:, tb * D + 512:(tb + 1) * D],
                             in_=psCD[bk][:, 0:512],
                             func=AF.Sigmoid).then_inc(sch.h, 1)
            P.s(s_sig)
            t_sg = sch.inc(1)

            def v_free(e, t_sg=t_sg):
                e.wait_ge(sch.h, t_sg)
                e.engine_nop().then_inc(vch.h, 1)
            P.v(v_free)
            bank_free[bk] = vch.inc(1)
            state[f"er_{si}_{tb}"] = bank_free[bk]
        t_r_all_s = sch.n

        for ech in range(NCH):
            bk = ech % 2
            t_bankfree = bank_free[bk]
            t_tmp_used = state.get("ek2_last", 0)

            def t_wk_mm(e, ech=ech, bk=bk, t_mix=t_mix, t_w=t_wk,
                        t_bankfree=t_bankfree):
                e.wait_ge(vch.h, max(t_mix, t_w, t_bankfree))
                for c in range(NCH):
                    ins = e.matmul(out=psAB[bk][:, 0:512],
                                   lhsT=wk_bf[:, c * D + ech * 128:c * D + (ech + 1) * 128],
                                   rhs=xk_fm[:, c * 512:(c + 1) * 512],
                                   start=(c == 0), stop=(c == NCH - 1))
                ins.then_inc(tch.h, 1)
            P.t(t_wk_mm)
            t_mm = tch.inc(1)

            def v_relu(e, ech=ech, bk=bk, t_mm=t_mm, t_tmp_used=t_tmp_used):
                e.wait_ge(tch.h, t_mm)
                wge(e, vch.h, t_tmp_used)
                e.tensor_scalar_max(out=tmpE[:, 0:512], in0=psAB[bk][:, 0:512],
                                    scalar1=0.0).then_inc(vch.h, 1)
            P.v(v_relu)
            t_rl = vch.inc(1)
            bank_free[bk] = t_rl

            def v_sq(e, ech=ech, t_rl=t_rl):
                e.wait_ge(vch.h, t_rl)
                e.tensor_tensor(out=k2_fm[:, ech * 512:(ech + 1) * 512],
                                in0=tmpE[:, 0:512], in1=tmpE[:, 0:512],
                                op=ALU.mult).then_inc(vch.h, 1)
            P.v(v_sq)
            state["ek2_last"] = vch.inc(1)
        t_k2 = state["ek2_last"]

        for tb in range(4):
            bk = tb % 2
            t_bankfree = bank_free[bk]

            def t_wv_mm(e, tb=tb, bk=bk, t_k2=t_k2, t_w=t_wv,
                        t_bankfree=t_bankfree):
                e.wait_ge(vch.h, max(t_k2, t_w, t_bankfree))
                for ech in range(NCH):
                    e.matmul(out=psAB[bk][:, 0:512],
                             lhsT=k2_fm[:, ech * 512 + tb * 128:ech * 512 + (tb + 1) * 128],
                             rhs=wv_bf[:, ech * D:ech * D + 512],
                             start=(ech == 0), stop=(ech == NCH - 1))
                    ins = e.matmul(out=psCD[bk][:, 0:512],
                                   lhsT=k2_fm[:, ech * 512 + tb * 128:ech * 512 + (tb + 1) * 128],
                                   rhs=wv_bf[:, ech * D + 512:ech * D + 1024],
                                   start=(ech == 0), stop=(ech == NCH - 1))
                ins.then_inc(tch.h, 1)
            P.t(t_wv_mm)
            t_mm = tch.inc(1)
            state[f"ekv_ev_{si}_{tb}"] = t_mm

            def v_kv(e, tb=tb, bk=bk, t_mm=t_mm, t_rs=t_r_all_s,
                     t_s4=s4_stored):
                e.wait_ge(tch.h, t_mm)
                e.wait_ge(sch.h, t_rs)
                wge(e, st_c.h, t_s4)
                e.tensor_tensor(out=out2[:, tb * D:tb * D + 512],
                                in0=psAB[bk][:, 0:512],
                                in1=r_tm[:, tb * D:tb * D + 512], op=ALU.mult)
                e.tensor_tensor(out=out2[:, tb * D + 512:(tb + 1) * D],
                                in0=psCD[bk][:, 0:512],
                                in1=r_tm[:, tb * D + 512:(tb + 1) * D],
                                op=ALU.mult).then_inc(vch.h, 1)
            P.v(v_kv)
            bank_free[bk] = vch.inc(1)
        t_out2 = bank_free[(4 - 1) % 2]

        def v_pre2(e, t_out2=t_out2, t_o2=bank_free[0]):
            e.wait_ge(vch.h, max(t_out2, t_o2))
        P.v(v_pre2)
        t_ap2 = emit_ln(P, lnb, vch, sch, nblk=4,
                        src=out2, srcol=lambda j: j * D,
                        dst=out2, dstcol=lambda j: j * D)

        def sy_xr(e, t0=t0, t_s4=s4_stored):
            wge(e, st_c.h, t_s4)
            e.dma_start(out=xres.rearrange("p (j c) -> p j c", j=4),
                        in_=s3e[t0 + 1:t0 + 513, :].rearrange(
                            "(j p) c -> p j c", p=128)).then_inc(ldxr.h, 16)
        P.sy(sy_xr)
        t_xr = ldxr.inc(16)

        def v_res(e, t_ap2=t_ap2, t_xr=t_xr, t_s4=s4_stored):
            e.wait_ge(sch.h, t_ap2)
            e.wait_ge(ldxr.h, t_xr)
            wge(e, st_c.h, t_s4)
            e.tensor_tensor(out=out2[:, 0:4 * D], in0=out2[:, 0:4 * D],
                            in1=xres[:, 0:4 * D], op=ALU.add).then_inc(vch.h, 1)
        P.v(v_res)
        t_rv = vch.inc(1)

        def sy_s4(e, t0=t0, t_rv=t_rv):
            e.wait_ge(vch.h, t_rv)
            e.dma_start(out=s4[t0:t0 + 512, :].rearrange("(j p) c -> p j c", p=128),
                        in_=out2.rearrange("p (j c) -> p j c", j=4)
                        ).then_inc(st_c.h, 16)
        P.sy(sy_s4)
        s4_stored = st_c.inc(16)
        state[f"emix_used_{si}"] = state[f"ekv_ev_{si}_{3}"]

    def sy_fin(e, t=s4_stored):
        e.wait_ge(st_c.h, t)
    P.sy(sy_fin)

    block = es.enter_context(nc.Block())
    P.emit(block)
    es.close()
    return nc



_B, _T, _DM = 4, 8192, 1024
_TO = _T // 2

_cache = {}


def _programs():
    if "A" not in _cache:
        _cache["A"] = build_l0attn(_TO, _T)
        _cache["B"] = build_l0ffn(_TO)
        _cache["C"] = build_l1state(_TO)
        _cache["D"] = build_l1attn(_TO)
        _cache["E"] = build_l1ffn(_TO)
    return _cache


def _tr(w):
    return np.ascontiguousarray(np.asarray(w).T.astype(np.float32))


def kernel(L0_attn_x, L0_attn_Wq, L0_attn_Wk, L0_attn_Wv,
           L0_ffn_w1, L0_ffn_conv_w, L0_ffn_conv_b, L0_ffn_w2,
           L1_attn_Wq, L1_attn_bq, L1_attn_Wk, L1_attn_bk, L1_attn_Wv,
           L1_attn_bv, L1_attn_Wo, L1_attn_bo,
           L1_ffn_W_r, L1_ffn_W_k, L1_ffn_W_v, L1_ffn_mu_r, L1_ffn_mu_k):
    progs = _programs()
    x = np.ascontiguousarray(np.asarray(L0_attn_x, dtype=np.float32))
    cores = list(range(8))

    wqT = _tr(L0_attn_Wq); wkT = _tr(L0_attn_Wk); wvT = _tr(L0_attn_Wv)
    in_a = []
    for c in cores:
        b, h = c // 2, c % 2
        in_a.append(dict(x_full=x[b], x_own=x[b, h * _TO:(h + 1) * _TO],
                         wqT=wqT, wkT=wkT, wvT=wvT))
    res = run_bass_kernel_spmd(progs["A"], in_a, core_ids=cores)
    s1 = [np.asarray(r["s1"]) for r in res.results]

    w1t = _tr(L0_ffn_w1)
    cw = np.ascontiguousarray(np.asarray(L0_ffn_conv_w, dtype=np.float32
                                         ).reshape(4096, 3))
    cb = np.ascontiguousarray(np.asarray(L0_ffn_conv_b, dtype=np.float32))
    w2t = _tr(L0_ffn_w2)
    z = np.zeros((1, _DM), np.float32)
    in_b = []
    for c in cores:
        b, h = c // 2, c % 2
        prev = z if h == 0 else s1[2 * b][-1:]
        nxt = s1[2 * b + 1][0:1] if h == 0 else z
        s1e = np.concatenate([prev, s1[c], nxt], 0)
        in_b.append(dict(s1e=s1e, w1t=w1t, cw=cw, cb=cb, w2t=w2t))
    res = run_bass_kernel_spmd(progs["B"], in_b, core_ids=cores)
    s2 = [np.asarray(r["s2"]) for r in res.results]

    wk2T = _tr(L1_attn_Wk); wv2T = _tr(L1_attn_Wv)
    bk2 = np.asarray(L1_attn_bk, dtype=np.float32)
    bv2 = np.asarray(L1_attn_bv, dtype=np.float32)
    in_c = [dict(s2=s2[c], wkT=wk2T, wvT=wv2T, bk=bk2, bv=bv2) for c in cores]
    res = run_bass_kernel_spmd(progs["C"], in_c, core_ids=cores)
    st_loc = [np.asarray(r["state_loc"]) for r in res.results]

    wq2T = _tr(L1_attn_Wq); wo2T = _tr(L1_attn_Wo)
    bq2 = np.asarray(L1_attn_bq, dtype=np.float32)
    bo2 = np.asarray(L1_attn_bo, dtype=np.float32)
    zsv = np.zeros((64, 1024), np.float32)
    zsk = np.zeros((64, 16), np.float32)
    in_d = []
    for c in cores:
        b, h = c // 2, c % 2
        if h == 0:
            sv, sk = zsv, zsk
        else:
            sl = st_loc[2 * b].reshape(16, 64, 65)
            sv = np.ascontiguousarray(sl[:, :, 0:64].transpose(1, 0, 2).reshape(64, 1024))
            sk = np.ascontiguousarray(sl[:, :, 64].T)
        in_d.append(dict(s2=s2[c], wqT=wq2T, wkT=wk2T, wvT=wv2T, woT=wo2T,
                         bq=bq2, bk=bk2, bv=bv2, bo=bo2, state_v=sv, state_k=sk))
    res = run_bass_kernel_spmd(progs["D"], in_d, core_ids=cores)
    s3 = [np.asarray(r["s3"]) for r in res.results]

    wrT = _tr(L1_ffn_W_r); wk3T = _tr(L1_ffn_W_k); wv3T = _tr(L1_ffn_W_v)
    mur = np.asarray(L1_ffn_mu_r, dtype=np.float32)
    muk = np.asarray(L1_ffn_mu_k, dtype=np.float32)
    in_e = []
    for c in cores:
        b, h = c // 2, c % 2
        prev = z if h == 0 else s3[2 * b][-1:]
        s3e = np.concatenate([prev, s3[c]], 0)
        in_e.append(dict(s3e=s3e, wrT=wrT, wkT=wk3T, wvT=wv3T,
                         mu_r=mur, mu_k=muk))
    res = run_bass_kernel_spmd(progs["E"], in_e, core_ids=cores)
    s4 = [np.asarray(r["s4"]) for r in res.results]

    out = np.empty((_B, _T, _DM), np.float32)
    for c in cores:
        b, h = c // 2, c % 2
        out[b, h * _TO:(h + 1) * _TO] = s4[c]
    return out
